# version 04
# baseline (speedup 1.0000x reference)
import numpy as np
import concourse.bass as bass
import concourse.mybir as mybir
from concourse.bass_utils import run_bass_kernel_spmd

F32 = mybir.dt.float32
BF16 = mybir.dt.bfloat16
AF = mybir.ActivationFunctionType
ALU = mybir.AluOpType
AX = mybir.AxisListType

T, D, H, HD, DC = 2048, 1024, 8, 128, 8
C = 128
NCH = T // C
TILE = 512
NTILE = T // TILE
CPT = TILE // C
NE = 8
NEG = -1.0e30

ENGS = ("tensor", "vector", "scalar", "gpsimd", "sync")
MILESTONES = ("pe", "dv", "ac", "gp")


class Sched:
    def __init__(self, nc):
        self.nc = nc
        self.q = {e: [] for e in ENGS}
        self.sems = {}
        self.cnt = {}
        self.bank_free = {}
        self.dma_fams = {}

    def sem(self, name):
        if name not in self.sems:
            self.sems[name] = self.nc.alloc_semaphore(name=name)
            self.cnt[name] = 0
        return self.sems[name]

    def wait(self, eng, name, val=None):
        sm = self.sem(name)
        v = self.cnt[name] if val is None else val
        if v > 0:
            self.q[eng].append(lambda e, sm=sm, v=v: e.wait_ge(sm, v))

    def add(self, eng, fn, inc=None, inck=1):
        if inc is None:
            self.q[eng].append(fn)
            return None
        sm = self.sem(inc)
        self.cnt[inc] += inck

        def g(e, fn=fn, sm=sm, k=inck):
            fn(e).then_inc(sm, k)

        self.q[eng].append(g)
        return self.cnt[inc]

    def dma(self, eng, fn, fam):
        st = self.dma_fams.setdefault(fam, {"k": 8, "i": 0})
        idx = st["i"] % st["k"]
        st["i"] += 1
        name = f"{fam}{idx}"
        self.sem(name)
        cur = self.cnt[name]
        if cur > 0:
            self.wait(eng, name, cur)
        v = self.add(eng, fn, inc=name, inck=16)
        return (name, v)

    def wait_fam(self, eng, fam):
        st = self.dma_fams.get(fam)
        if not st:
            return
        for i in range(st["k"]):
            name = f"{fam}{i}"
            if name in self.cnt:
                self.wait(eng, name)

    def pe_acquire(self, bank):
        for (sname, v) in self.bank_free.get(bank, ()):
            self.wait("tensor", sname, v)
        self.bank_free[bank] = []

    def bank_release(self, bank, semname):
        self.bank_free.setdefault(bank, []).append((semname, self.cnt[semname]))

    def barrier(self):
        for e in ENGS:
            for n in MILESTONES:
                if n in self.cnt:
                    self.wait(e, n)
            for fam in self.dma_fams:
                self.wait_fam(e, fam)


class SB:
    def __init__(self, nc, base=16512, top=229344):
        self.nc, self.cur, self.top = nc, base, top

    def alloc(self, name, shape, dtype):
        per = int(np.prod(shape[1:])) * mybir.dt.size(dtype)
        off = (self.cur + 31) // 32 * 32
        t = self.nc.alloc_sbuf_tensor_at(name, list(shape), dtype, offset=off)
        self.cur = off + per
        assert self.cur <= self.top, f"SBUF overflow at {name}: {self.cur}>{self.top}"
        return t

    def mark(self):
        return self.cur

    def reset(self, m):
        self.cur = m


def build(upto="full", dumps=(), p2_tiles=NTILE, p2_chunks=CPT, p2_heads=H, p2_ow=True, p2_opost=True, headops=8, g_skip=False, g_mode=0, p4_tcg=NCH, p4_experts=NE):
    nc = bass.Bass("TRN2", target_bir_lowering=False, detect_race_conditions=False)

    def din(name, shape):
        return nc.dram_tensor(name, list(shape), F32, kind="ExternalInput")

    xT = din("xT", [D, T])
    qw, kw, vw, gw, ow = (din(n, [D, D]) for n in ("qw", "kw", "vw", "gw", "ow"))
    bw, aw = din("bw", [D, H]), din("aw", [D, H])
    fcw, pw = din("fcw", [D, 4 * D]), din("pw", [4 * D, D])
    w1, w2 = din("w1", [NE, D, 4 * D]), din("w2", [NE, 4 * D, D])
    b1t, b2 = din("b1t", [NE, 128, 32]), din("b2", [NE, D])
    ln1c_h, ln2c_h = din("ln1c", [128, 8]), din("ln2c", [128, 8])
    onorm_r = din("onorm_r", [1, HD])
    alog_r, dtb_r = din("alog_r", [1, H]), din("dtb_r", [1, H])
    rwt, lwt = din("rwt", [D, NE]), din("lwt", [D, 4])
    rb_r, lb_r = din("rb_r", [1, NE]), din("lb_r", [1, 4])
    y_h = nc.dram_tensor("y", [D, T], F32, kind="ExternalOutput")
    scr_cb = nc.dram_tensor("scr_cb", [1, 2048], F32, kind="Internal")
    scr_we = nc.dram_tensor("scr_we", [NE, T], BF16, kind="Internal")

    eye_d = nc.inline_tensor(np.eye(128, dtype=np.float32), "eye_d")
    triu_d = nc.inline_tensor(np.triu(np.ones((128, 128), np.float32)), "triu_d")
    ii, jj = np.indices((128, 128))
    mus_d = nc.inline_tensor(np.where(jj > ii, 0.0, NEG).astype(np.float32), "mus_d")
    mls_d = nc.inline_tensor(np.where(jj < ii, 0.0, NEG).astype(np.float32), "mls_d")
    mui_d = nc.inline_tensor(np.where(jj >= ii, 0.0, NEG).astype(np.float32), "mui_d")

    pb = [nc.alloc_psum_tensor(f"pb{i}", [128, 512], F32) for i in range(7)]
    pt = nc.alloc_psum_tensor("pt", [128, 1024], BF16)

    sba = SB(nc)
    ACC = sba.alloc("ACC", [128, DC, T], F32)
    XN = sba.alloc("XN", [128, DC, T], BF16)
    eye_f = sba.alloc("eye_f", [128, 128], F32)
    eye_b = sba.alloc("eye_b", [128, 128], BF16)
    triu_f = sba.alloc("triu_f", [128, 128], F32)
    musf = sba.alloc("musf", [128, 128], F32)
    mlsf = sba.alloc("mlsf", [128, 128], F32)
    muif = sba.alloc("muif", [128, 128], F32)
    ones1 = sba.alloc("ones1", [128, 1], F32)
    onesr_f = sba.alloc("onesr_f", [1, 128], F32)
    onesr_b = sba.alloc("onesr_b", [1, 128], BF16)
    epsln = sba.alloc("epsln", [1, 1], F32)
    eps6 = sba.alloc("eps6", [1, 1], F32)
    eps6c = sba.alloc("eps6c", [128, 1], F32)
    one1 = sba.alloc("one1", [128, 1], F32)
    ln1c = sba.alloc("ln1c", [128, 8], F32)
    ln2c = sba.alloc("ln2c", [128, 8], F32)
    onorm_bc = sba.alloc("onorm_bc", [128, 128], F32)
    dtb_bc = sba.alloc("dtb_bc", [128, 8], F32)
    nega_bc = sba.alloc("nega_bc", [128, 8], F32)
    rlw = sba.alloc("rlw", [128, DC, 12], BF16)
    rlb_bc = sba.alloc("rlb_bc", [128, 12], F32)
    arow = sba.alloc("arow", [1, 16], F32)
    l2s = sba.alloc("l2s", [128, 8], F32)
    rowsc = sba.alloc("rowsc", [1, 512], F32)
    rows2 = sba.alloc("rows2", [1, 512], F32)
    rowt = sba.alloc("rowt", [1, 512], F32)
    sq0 = sba.alloc("sq0", [128, 512], F32)
    sq1 = sba.alloc("sq1", [128, 512], F32)
    l2q = sba.alloc("l2q", [128, 1024], F32)

    ph = sba.mark()
    DUMPF = None
    ph2 = sba.mark()
    XF = sba.alloc("XF", [128, DC, T], F32)
    sba.reset(ph2)
    QB = sba.alloc("QB", [128, CPT, 1024], BF16)
    KB = sba.alloc("KB", [128, CPT, 1024], BF16)
    VB = sba.alloc("VB", [128, CPT, 1024], BF16)
    GB = sba.alloc("GB", [128, CPT, 1024], BF16)
    HB = sba.alloc("HB", [128, DC, TILE], BF16)
    OTM = sba.alloc("OTM", [128, 1024], F32)
    OFM = sba.alloc("OFM", [128, DC, TILE], BF16)
    WR0 = sba.alloc("WR0", [128, DC, 512], BF16)
    WR1 = sba.alloc("WR1", [128, DC, 512], BF16)
    BAW = sba.alloc("BAW", [128, DC, 16], BF16)
    S_f = sba.alloc("S_f", [128, H, HD], F32)
    S_b = sba.alloc("S_b", [128, H, HD], BF16)
    cbg = sba.alloc("cbg", [128, CPT, 24], F32)
    cb_fm = sba.alloc("cb_fm", [16, 128], F32)
    cb_rows = sba.alloc("cb_rows", [1, 2048], F32)
    bneg_rb = sba.alloc("bneg_rb", [1, 1024], BF16)
    qfm = sba.alloc("qfm", [128, 128], BF16)
    kfm = sba.alloc("kfm", [128, 128], BF16)
    qfm1 = sba.alloc("qfm1", [128, 128], BF16)
    kfm1 = sba.alloc("kfm1", [128, 128], BF16)
    kdec = sba.alloc("kdec", [128, 128], BF16)
    bm = sba.alloc("bm", [128, 128], BF16)
    bmt = sba.alloc("bmt", [128, 128], BF16)
    mqt = sba.alloc("mqt", [128, 128], BF16)
    u0 = sba.alloc("u0", [128, 128], BF16)
    u1 = sba.alloc("u1", [128, 128], BF16)
    pp = [sba.alloc(f"pp{i}", [128, 128], BF16) for i in range(2)]
    ppt = [sba.alloc(f"ppt{i}", [128, 128], BF16) for i in range(2)]
    negd = sba.alloc("negd", [128, 128], F32)
    dml = sba.alloc("dml", [128, 128], F32)
    dmus = sba.alloc("dmus", [128, 128], F32)
    dmui = sba.alloc("dmui", [128, 128], F32)
    x1s = sba.alloc("x1s", [128, 128], F32)
    gamc = sba.alloc("gamc", [128, 1], F32)
    gamCc = sba.alloc("gamCc", [128, 1], F32)
    decc = sba.alloc("decc", [128, 1], F32)
    osb = sba.alloc("osb", [128, 128], F32)
    sba.reset(ph2)
    H2 = sba.alloc("H2", [128, DC, T], BF16)
    p34 = sba.mark()
    HFC = sba.alloc("HFC", [128, 8, T], BF16)
    W2Q = sba.alloc("W2Q", [128, 8, 1024], BF16)
    FR0 = sba.alloc("FR0", [128, DC, 256], BF16)
    FR1 = sba.alloc("FR1", [128, DC, 256], BF16)
    sba.reset(ph2)
    WEBC = sba.alloc("WEBC", [128, T], BF16)
    HES0 = sba.alloc("HES0", [128, 512], F32)
    HES1 = sba.alloc("HES1", [128, 512], F32)
    werow = sba.alloc("werow", [1, 2048], BF16)
    b2row = sba.alloc("b2row", [1, 1024], BF16)
    b1col = sba.alloc("b1col", [128, 32], F32)
    p4r = sba.mark()
    wetm = sba.alloc("wetm", [128, NCH, 8], F32)
    wefm = sba.alloc("wefm", [8, T], BF16)
    sba.reset(p4r)
    W2Q1 = sba.alloc("W2Q1", [128, 8, 1024], BF16)
    assert sba.mark() <= p34, "P4 overlay exceeds H2 space"
    sba.reset(ph2)
    XR = sba.alloc("XR", [128, 2, T], F32)
    sba.reset(ph)

    s = Sched(nc)
    for n in MILESTONES:
        s.sem(n)
    bank_rot = [0]

    def rot():
        b = bank_rot[0]
        bank_rot[0] = (bank_rot[0] + 1) % 4
        return b

    for dst, src in ((eye_f, eye_d), (triu_f, triu_d), (musf, mus_d),
                     (mlsf, mls_d), (muif, mui_d)):
        s.dma("sync", lambda e, dst=dst, src=src: e.dma_start(dst[:], src.ap()), "dx")
    s.dma("sync", lambda e: e.dma_start(ln1c[:], ln1c_h.ap()), "dx")
    s.dma("sync", lambda e: e.dma_start(ln2c[:], ln2c_h.ap()), "dx")
    s.dma("sync", lambda e: e.dma_start(arow[0:1, 0:8], alog_r.ap()), "dx")
    s.dma("sync", lambda e: e.dma_start(arow[0:1, 8:16], dtb_r.ap()), "dx")
    s.dma("sync", lambda e: e.dma_start(rlb_bc[0:1, 0:8], rb_r.ap()), "dx")
    s.dma("sync", lambda e: e.dma_start(rlb_bc[0:1, 8:12], lb_r.ap()), "dx")
    s.dma("sync", lambda e: e.dma_start(rowt[0:1, 0:128], onorm_r.ap()), "dx")
    s.dma("gpsimd", lambda e: e.dma_start(
        rlw[:, :, 0:8], rwt.ap().rearrange("(dc p) e -> p dc e", p=128)), "dw")
    s.dma("gpsimd", lambda e: e.dma_start(
        rlw[:, :, 8:12], lwt.ap().rearrange("(dc p) e -> p dc e", p=128)), "dw")
    for tile_, v_ in ((ones1, 1.0), (onesr_f, 1.0), (onesr_b, 1.0), (epsln, 1e-5),
                      (eps6, 1e-6), (eps6c, 1e-6), (one1, 1.0)):
        s.add("gpsimd", lambda e, t=tile_, v=v_: e.memset(t[:], v),
              inc="gp" if tile_ is one1 else None)
    s.wait_fam("scalar", "dx")
    s.wait_fam("scalar", "dw")
    s.wait("scalar", "gp")
    s.add("scalar", lambda e: e.activation(eye_b[:], eye_f[:], AF.Copy), inc="ac")
    s.add("scalar", lambda e: e.activation(arow[0:1, 0:8], arow[0:1, 0:8], AF.Exp), inc="ac")
    s.wait("vector", "ac")
    s.add("vector", lambda e: e.tensor_scalar_mul(arow[0:1, 0:8], arow[0:1, 0:8], -1.0), inc="dv")
    s.wait("tensor", "dv")
    s.wait_fam("tensor", "dx")
    s.wait_fam("tensor", "dw")
    s.add("tensor", lambda e: e.matmul(pb[4][:, 0:16], onesr_f[:], arow[:], start=True, stop=True), inc="pe")
    s.add("tensor", lambda e: e.matmul(pb[5][:, 0:128], onesr_f[:], rowt[0:1, 0:128], start=True, stop=True), inc="pe")
    s.add("tensor", lambda e: e.matmul(pb[6][:, 0:12], onesr_f[:], rlb_bc[0:1, 0:12], start=True, stop=True), inc="pe")
    s.wait("vector", "pe")
    s.add("vector", lambda e: e.tensor_copy(nega_bc[:], pb[4][:, 0:8]))
    s.add("vector", lambda e: e.tensor_copy(dtb_bc[:], pb[4][:, 8:16]))
    s.add("vector", lambda e: e.tensor_copy(onorm_bc[:], pb[5][:, 0:128]))
    s.add("vector", lambda e: e.tensor_copy(rlb_bc[:], pb[6][:, 0:12]), inc="dv")
    for b_ in (4, 5, 6):
        s.bank_release(b_, "dv")

    xf_ld = []
    for dc in range(DC):
        xf_ld.append(s.dma("sync", lambda e, dc=dc:
                           e.dma_start(XF[:, dc, :], xT.ap()[dc * 128:(dc + 1) * 128, :]), "dx"))

    for nt0 in range(0, T, 512):
        a, b = nt0, nt0 + 512
        pe_sq = {}
        s.pe_acquire(2)
        s.pe_acquire(3)
        s.wait("scalar", "dv")
        for dc in range(DC):
            sqb = sq0 if dc % 2 == 0 else sq1
            if dc >= 2:
                s.wait("scalar", "pe", pe_sq[dc - 2])
            s.wait("scalar", xf_ld[dc][0], xf_ld[dc][1])
            s.add("scalar", lambda e, dc=dc, a=a, b=b, sqb=sqb:
                  e.activation(sqb[:], XF[:, dc, a:b], AF.Square), inc="ac")
            s.wait("tensor", "ac")
            s.add("tensor", lambda e, dc=dc, a=a, b=b:
                  e.matmul(pb[2][0:1, 0:512], ones1[:], XF[:, dc, a:b],
                           start=(dc == 0), stop=(dc == DC - 1)))
            pe_sq[dc] = s.add("tensor", lambda e, dc=dc, sqb=sqb:
                              e.matmul(pb[3][0:1, 0:512], ones1[:], sqb[:],
                                       start=(dc == 0), stop=(dc == DC - 1)), inc="pe")
        s.wait("vector", "pe")
        s.add("vector", lambda e, a=a, b=b:
              e.tensor_scalar_mul(rowsc[0:1, 0:512], pb[2][0:1, 0:512], 1.0 / D), inc="dv")
        s.add("vector", lambda e:
              e.tensor_scalar(rowt[0:1, 0:512], pb[3][0:1, 0:512], 1.0 / D, None,
                              ALU.mult, ALU.bypass), inc="dv")
        s.bank_release(2, "dv")
        s.bank_release(3, "dv")
        s.wait("scalar", "dv")
        s.add("scalar", lambda e, a=a, b=b:
              e.activation(sq0[0:1, 0:512], rowsc[0:1, 0:512], AF.Square), inc="ac")
        s.wait("vector", "ac")
        s.add("vector", lambda e:
              e.tensor_tensor(rowt[0:1, 0:512], rowt[0:1, 0:512], sq0[0:1, 0:512],
                              ALU.subtract), inc="dv")
        s.wait("scalar", "dv")
        s.add("scalar", lambda e:
              e.activation(rowt[0:1, 0:512], rowt[0:1, 0:512], AF.Ln, bias=epsln[:]))
        s.add("scalar", lambda e, a=a, b=b:
              e.activation(rows2[0:1, 0:512], rowt[0:1, 0:512], AF.Exp, scale=-0.5), inc="ac")
        s.pe_acquire(4)
        s.pe_acquire(5)
        s.wait("tensor", "ac")
        s.add("tensor", lambda e, a=a, b=b:
              e.matmul(pb[4][:, 0:512], onesr_f[:], rowsc[0:1, 0:512], start=True, stop=True), inc="pe")
        s.add("tensor", lambda e, a=a, b=b:
              e.matmul(pb[5][:, 0:512], onesr_f[:], rows2[0:1, 0:512], start=True, stop=True), inc="pe")
        s.wait("vector", "pe")
        s.wait("scalar", "pe")
        for dc in range(DC):
            sqb = sq0 if dc % 2 == 0 else sq1
            s.add("vector", lambda e, dc=dc, a=a, b=b, sqb=sqb:
                  e.tensor_tensor(sqb[:], XF[:, dc, a:b], pb[4][:, 0:512], ALU.subtract))
            s.add("vector", lambda e, dc=dc, a=a, b=b, sqb=sqb:
                  e.tensor_tensor(ACC[:, dc, a:b], sqb[:], pb[5][:, 0:512], ALU.mult), inc="dv")
            s.wait("scalar", "dv")
            s.add("scalar", lambda e, dc=dc, a=a, b=b:
                  e.activation(XN[:, dc, a:b], ACC[:, dc, a:b], AF.Copy), inc="ac")
        s.bank_release(4, "dv")
        s.bank_release(5, "dv")
    s.barrier()

    if upto == "p1":
        return finish(nc, s, y_h, ACC, dumps, locals())

    wslab_state = {}
    WRS = (WR0, WR1)
    slot_i = [0]

    def next_slot():
        sl = slot_i[0]
        slot_i[0] ^= 1
        return sl

    def wslab_dma(slot, fn):
        prev = wslab_state.get(slot)
        if prev is not None:
            s.wait("gpsimd", "pe", prev)
        return s.dma("gpsimd", fn, "dw")

    def proj_tensor(wt_h, dst_tile, act):
        for si in range(2):
            j0 = si * 512
            slot = next_slot()
            dwv = wslab_dma(slot, lambda e, j0=j0, slot=slot, wt_h=wt_h:
                            e.dma_start(WRS[slot][:],
                                        wt_h.ap()[:, j0:j0 + 512].rearrange("(dc p) j -> p dc j", p=128)))
            s.wait("tensor", dwv[0], dwv[1])
            for tc in range(CPT):
                bank = rot()
                s.pe_acquire(bank)
                for dc in range(DC):
                    r = s.add("tensor", lambda e, dc=dc, tc=tc, slot=slot, bank=bank:
                              e.matmul(pb[bank][:, 0:512],
                                       HB[:, dc, tc * 128:(tc + 1) * 128],
                                       WRS[slot][:, dc, :],
                                       start=(dc == 0), stop=(dc == DC - 1)),
                              inc="pe" if dc == DC - 1 else None)
                wslab_state[slot] = s.cnt["pe"]
                s.wait("scalar", "pe", s.cnt["pe"])
                s.add("scalar", lambda e, tc=tc, j0=j0, bank=bank, dst_tile=dst_tile:
                      e.activation(dst_tile[:, tc, j0:j0 + 512], pb[bank][:, 0:512], act), inc="ac")
                s.bank_release(bank, "ac")

    QKF = ((qfm, kfm), (qfm1, kfm1))

    def pre_head(tc, h, par):
        qf, kf = QKF[par]

        def t1():
            s.pe_acquire(7)
            s.add("tensor", lambda e, tc=tc, h=h:
                  e.transpose(pt[:, 0:128], QB[:, tc, h * 128:(h + 1) * 128], eye_b[:]))
            s.add("tensor", lambda e, tc=tc, h=h:
                  e.transpose(pt[:, 128:256], KB[:, tc, h * 128:(h + 1) * 128], eye_b[:]), inc="pe")
            s.wait("scalar", "pe")
            s.add("scalar", lambda e, qf=qf: e.activation(qf[:], pt[:, 0:128], AF.Copy))
            s.add("scalar", lambda e, kf=kf: e.activation(kf[:], pt[:, 128:256], AF.Copy), inc="ac")
            s.bank_release(7, "ac")

        def t2():
            s.pe_acquire(2)
            s.add("tensor", lambda e, h=h:
                  e.matmul(pb[2][:, 0:128], onesr_f[:], cb_rows[0:1, h * 128:(h + 1) * 128],
                           start=True, stop=True))
            s.add("tensor", lambda e, h=h:
                  e.matmul(pb[2][:, 128:256], onesr_b[:], bneg_rb[0:1, h * 128:(h + 1) * 128],
                           start=True, stop=True))
            s.add("tensor", lambda e, h=h:
                  e.matmul(pb[2][:, 256:257], onesr_f[:],
                           cb_rows[0:1, h * 128 + 127:h * 128 + 128], start=True, stop=True), inc="pe")

        def t3():
            s.pe_acquire(3)
            s.wait("tensor", "ac")
            s.add("tensor", lambda e, kf=kf: e.matmul(pb[3][:, 0:128], kf[:], kf[:], start=True, stop=True))
            s.add("tensor", lambda e, kf=kf, qf=qf:
                  e.matmul(pb[3][:, 128:256], kf[:], qf[:], start=True, stop=True), inc="pe")

        return [t1, t2, t3]

    def emit_head(tc, ci, h, otm, par, thunks):
        qfm_, kfm_ = QKF[par]
        qfm, kfm = qfm_, kfm_
        s.pe_acquire(4)
        s.add("tensor", lambda e, h=h: e.matmul(pb[4][:, 0:128], kfm[:], S_b[:, h, :], start=True, stop=True))
        s.add("tensor", lambda e, h=h: e.matmul(pb[4][:, 128:256], qfm[:], S_b[:, h, :], start=True, stop=True), inc="pe")
        pe_all = s.cnt["pe"]
        s.wait("vector", "pe", pe_all)
        s.wait("scalar", "pe", pe_all)
        s.add("vector", lambda e, tc=tc, h=h:
              e.tensor_scalar(negd[:], pb[2][:, 0:128], cbg[:, tc, h:h + 1], None,
                              ALU.subtract, ALU.bypass))
        s.add("vector", lambda e, tc=tc, h=h:
              e.tensor_scalar(gamCc[:], pb[2][:, 256:257], cbg[:, tc, h:h + 1], None,
                              ALU.subtract, ALU.bypass), inc="dv")
        s.wait("scalar", "dv")
        s.add("scalar", lambda e, tc=tc, h=h: e.activation(gamc[:], cbg[:, tc, h:h + 1], AF.Exp))
        s.add("scalar", lambda e: e.activation(decc[:], gamCc[:], AF.Exp), inc="ac")
        s.wait("scalar", "ac")
        s.add("scalar", lambda e: e.activation(gamCc[:], pb[2][:, 256:257], AF.Exp), inc="ac")
        s.wait("vector", "ac")
        s.add("vector", lambda e: e.tensor_tensor(dmus[:], negd[:], musf[:], ALU.add))
        s.add("vector", lambda e: e.tensor_tensor(dmui[:], negd[:], muif[:], ALU.add))
        s.add("vector", lambda e: e.tensor_tensor(dml[:], mlsf[:], negd[:], ALU.subtract), inc="dv")
        s.wait("vector", "dv")
        s.wait("scalar", "dv")
        s.add("scalar", lambda e: e.activation(dmus[:], dmus[:], AF.Exp))
        s.add("scalar", lambda e: e.activation(dmui[:], dmui[:], AF.Exp))
        s.add("scalar", lambda e: e.activation(dml[:], dml[:], AF.Exp), inc="ac")
        s.wait("vector", "ac")
        s.add("vector", lambda e: e.tensor_tensor(x1s[:], pb[3][:, 0:128], dml[:], ALU.mult), inc="dv")
        s.wait("vector", "dv")
        s.add("vector", lambda e, tc=tc, h=h:
              e.tensor_scalar(bm[:], x1s[:], cbg[:, tc, 8 + h:9 + h], -1.0, ALU.mult, ALU.mult), inc="dv")
        s.wait("vector", "dv")
        s.add("vector", lambda e: e.tensor_tensor(x1s[:], pb[3][:, 0:128], dmus[:], ALU.mult), inc="dv")
        s.wait("vector", "dv")
        s.add("vector", lambda e: e.tensor_tensor(bmt[:], x1s[:], pb[2][:, 128:256], ALU.mult))
        s.add("vector", lambda e: e.tensor_tensor(mqt[:], pb[3][:, 128:256], dmui[:], ALU.mult), inc="dv")
        s.bank_release(3, "dv")
        s.bank_release(2, "dv")
        s.add("vector", lambda e: e.tensor_scalar_mul(x1s[:], pb[4][:, 0:128], gamc[:]), inc="dv")
        s.wait("vector", "dv")
        s.add("vector", lambda e, tc=tc, h=h:
              e.tensor_tensor(x1s[:], VB[:, tc, h * 128:(h + 1) * 128], x1s[:], ALU.subtract), inc="dv")
        s.wait("vector", "dv")
        s.add("vector", lambda e, tc=tc, h=h:
              e.tensor_scalar_mul(u0[:], x1s[:], cbg[:, tc, 8 + h:9 + h]))
        s.add("vector", lambda e, tc=tc, h=h:
              e.tensor_scalar_mul(kdec[:], KB[:, tc, h * 128:(h + 1) * 128], decc[:]), inc="dv")
        ucur, unext = u0, u1
        pcur, pcurt = bm, bmt
        nlev = 7
        s.wait("tensor", "dv")
        for lev in range(nlev):
            s.pe_acquire(6)
            s.add("tensor", lambda e, ucur=ucur:
                  e.matmul(pb[6][:, 256:384], eye_b[:], ucur[:], start=True, stop=False))
            s.add("tensor", lambda e, pcurt=pcurt, ucur=ucur:
                  e.matmul(pb[6][:, 256:384], pcurt[:], ucur[:], start=False, stop=True),
                  inc="pe" if lev == nlev - 1 else None)
            if lev < nlev - 1:
                s.pe_acquire(5)
                s.add("tensor", lambda e, pcur=pcur, pcurt=pcurt:
                      e.matmul(pb[5][:, 0:128], pcurt[:], pcur[:], start=True, stop=True))
                s.add("tensor", lambda e, pcur=pcur, pcurt=pcurt:
                      e.matmul(pb[5][:, 128:256], pcur[:], pcurt[:], start=True, stop=True), inc="pe")
            if lev < len(thunks):
                thunks[lev]()
            s.wait("scalar", "pe")
            s.add("scalar", lambda e, unext=unext: e.activation(unext[:], pb[6][:, 256:384], AF.Copy), inc="ac")
            s.bank_release(6, "ac")
            if lev < nlev - 1:
                pnext, pnextt = pp[lev % 2], ppt[lev % 2]
                s.add("scalar", lambda e, pnext=pnext: e.activation(pnext[:], pb[5][:, 0:128], AF.Copy))
                s.add("scalar", lambda e, pnextt=pnextt: e.activation(pnextt[:], pb[5][:, 128:256], AF.Copy), inc="ac")
                s.bank_release(5, "ac")
                pcur, pcurt = pnext, pnextt
            s.wait("tensor", "ac")
            ucur, unext = unext, ucur
        s.pe_acquire(6)
        s.wait("tensor", "dv")
        s.add("tensor", lambda e, ucur=ucur: e.matmul(pb[6][:, 0:128], mqt[:], ucur[:], start=True, stop=True))
        s.add("tensor", lambda e, ucur=ucur: e.matmul(pb[6][:, 128:256], kdec[:], ucur[:], start=True, stop=True), inc="pe")
        s.wait("scalar", "pe")
        s.add("scalar", lambda e: e.activation(osb[:], pb[6][:, 0:128], AF.Copy), inc="ac")
        s.wait("vector", "ac")
        s.add("vector", lambda e, h=h, otm=otm:
              e.scalar_tensor_tensor(otm[:, h * 128:(h + 1) * 128], pb[4][:, 128:256], gamc[:],
                                     osb[:], ALU.mult, ALU.add), inc="dv")
        s.bank_release(4, "dv")
        s.add("vector", lambda e, h=h:
              e.scalar_tensor_tensor(S_f[:, h, :], S_f[:, h, :], gamCc[:],
                                     pb[6][:, 128:256], ALU.mult, ALU.add), inc="dv")
        s.bank_release(6, "dv")
        s.wait("scalar", "dv")
        s.add("scalar", lambda e, h=h: e.activation(S_b[:, h, :], S_f[:, h, :], AF.Copy), inc="ac")
        s.wait("tensor", "ac")

    def l2norm_tensor(tens, tc):
        s.add("scalar", lambda e, tc=tc, tens=tens:
              e.activation(l2q[:], tens[:, tc, :], AF.Square), inc="ac")
        s.wait("vector", "ac")
        for h in range(H):
            s.add("vector", lambda e, h=h:
                  e.reduce_sum(l2s[:, h:h + 1], l2q[:, h * 128:(h + 1) * 128], AX.X),
                  inc="dv" if h == H - 1 else None)
        s.wait("scalar", "dv")
        s.add("scalar", lambda e: e.activation(l2s[:], l2s[:], AF.Ln, bias=eps6c[:]), inc="ac")
        s.wait("scalar", "ac")
        s.add("scalar", lambda e: e.activation(l2s[:], l2s[:], AF.Exp, scale=-0.5), inc="ac")
        s.wait("vector", "ac")
        for h in range(H):
            s.add("vector", lambda e, h=h, tc=tc, tens=tens:
                  e.tensor_scalar_mul(tens[:, tc, h * 128:(h + 1) * 128],
                                      tens[:, tc, h * 128:(h + 1) * 128], l2s[:, h:h + 1]),
                  inc="dv" if h == H - 1 else None)
        s.wait("scalar", "dv")
        s.wait("tensor", "dv")

    def o_post(tc, otm):
        s.wait("scalar", "dv")
        s.add("scalar", lambda e, otm=otm: e.activation(l2q[:], otm[:], AF.Square), inc="ac")
        s.wait("vector", "ac")
        for h in range(H):
            s.add("vector", lambda e, h=h:
                  e.reduce_sum(l2s[:, h:h + 1], l2q[:, h * 128:(h + 1) * 128], AX.X),
                  inc="dv" if h == H - 1 else None)
        s.wait("scalar", "dv")
        s.add("scalar", lambda e: e.activation(l2s[:], l2s[:], AF.Ln, bias=eps6c[:], scale=1.0 / HD), inc="ac")
        s.wait("scalar", "ac")
        s.add("scalar", lambda e: e.activation(l2s[:], l2s[:], AF.Exp, scale=-0.5), inc="ac")
        s.wait("vector", "ac")
        for h in range(H):
            s.add("vector", lambda e, h=h, otm=otm:
                  e.tensor_scalar_mul(otm[:, h * 128:(h + 1) * 128],
                                      otm[:, h * 128:(h + 1) * 128], l2s[:, h:h + 1]), inc="dv")
            s.wait("vector", "dv")
            s.add("vector", lambda e, h=h, otm=otm:
                  e.tensor_tensor(otm[:, h * 128:(h + 1) * 128],
                                  otm[:, h * 128:(h + 1) * 128], onorm_bc[:], ALU.mult), inc="dv")
            s.wait("vector", "dv")
            s.add("vector", lambda e, h=h, tc=tc, otm=otm:
                  e.tensor_tensor(l2q[:, h * 128:(h + 1) * 128],
                                  otm[:, h * 128:(h + 1) * 128],
                                  GB[:, tc, h * 128:(h + 1) * 128], ALU.mult),
                  inc="dv" if h == H - 1 else None)
        s.wait("scalar", "dv")
        s.add("scalar", lambda e, tc=tc: e.activation(VB[:, tc, :], l2q[:], AF.Copy), inc="ac")
        s.wait("tensor", "ac")
        for dc in range(DC):
            s.pe_acquire(7)
            s.add("tensor", lambda e, tc=tc, dc=dc:
                  e.transpose(pt[:, 256:384], VB[:, tc, dc * 128:(dc + 1) * 128], eye_b[:]), inc="pe")
            s.wait("scalar", "pe")
            s.add("scalar", lambda e, tc=tc, dc=dc:
                  e.activation(OFM[:, dc, tc * 128:(tc + 1) * 128], pt[:, 256:384], AF.Copy), inc="ac")
            s.bank_release(7, "ac")

    def attn_phase():
        bav1 = s.dma("gpsimd", lambda e: e.dma_start(
            BAW[:, :, 0:8], bw.ap().rearrange("(dc p) e -> p dc e", p=128)), "dw")
        bav2 = s.dma("gpsimd", lambda e: e.dma_start(
            BAW[:, :, 8:16], aw.ap().rearrange("(dc p) e -> p dc e", p=128)), "dw")
        s.add("gpsimd", lambda e: e.memset(S_f[:], 0.0))
        s.add("gpsimd", lambda e: e.memset(S_b[:], 0.0), inc="gp")
        s.wait("tensor", "gp")

        for ti in range(p2_tiles):
            t0 = ti * TILE
            a, b = t0, t0 + TILE
            pe_sq = {}
            s.pe_acquire(2)
            s.wait("scalar", "dv")
            for dc in range(DC):
                sqb = sq0 if dc % 2 == 0 else sq1
                if dc >= 2:
                    s.wait("scalar", "pe", pe_sq[dc - 2])
                s.add("scalar", lambda e, dc=dc, a=a, b=b, sqb=sqb:
                      e.activation(sqb[:], ACC[:, dc, a:b], AF.Square), inc="ac")
                s.wait("tensor", "ac")
                pe_sq[dc] = s.add("tensor", lambda e, dc=dc, sqb=sqb:
                                  e.matmul(pb[2][0:1, 0:512], ones1[:], sqb[:],
                                           start=(dc == 0), stop=(dc == DC - 1)), inc="pe")
            s.wait("vector", "pe")
            s.add("vector", lambda e:
                  e.tensor_scalar(rowt[0:1, 0:512], pb[2][0:1, 0:512], 1.0 / D, None,
                                  ALU.mult, ALU.bypass), inc="dv")
            s.bank_release(2, "dv")
            s.wait("scalar", "dv")
            s.add("scalar", lambda e:
                  e.activation(rowt[0:1, 0:512], rowt[0:1, 0:512], AF.Ln, bias=eps6[:]))
            s.add("scalar", lambda e:
                  e.activation(rowt[0:1, 0:512], rowt[0:1, 0:512], AF.Exp, scale=-0.5), inc="ac")
            s.pe_acquire(4)
            s.wait("tensor", "ac")
            s.add("tensor", lambda e:
                  e.matmul(pb[4][:, 0:512], onesr_f[:], rowt[0:1, 0:512], start=True, stop=True), inc="pe")
            s.wait("vector", "pe")
            for dc in range(DC):
                s.add("vector", lambda e, dc=dc, a=a, b=b:
                      e.tensor_tensor(sq0[:], ACC[:, dc, a:b], pb[4][:, 0:512], ALU.mult))
                s.add("vector", lambda e, dc=dc:
                      e.tensor_scalar_mul(HB[:, dc, :], sq0[:], ln1c[:, dc:dc + 1]),
                      inc="dv" if dc == DC - 1 else None)
            s.bank_release(4, "dv")
            s.wait("tensor", "dv")

            proj_tensor(qw, QB, AF.Silu)
            proj_tensor(kw, KB, AF.Silu)
            proj_tensor(vw, VB, AF.Silu)
            proj_tensor(gw, GB, AF.Silu)
            s.wait("tensor", bav1[0], bav1[1])
            s.wait("tensor", bav2[0], bav2[1])
            for tc in range(CPT):
                s.pe_acquire(6)
                for dc in range(DC):
                    s.add("tensor", lambda e, dc=dc, tc=tc:
                          e.matmul(pb[6][:, 0:16], HB[:, dc, tc * 128:(tc + 1) * 128],
                                   BAW[:, dc, :], start=(dc == 0), stop=(dc == DC - 1)),
                          inc="pe" if dc == DC - 1 else None)
                s.wait("scalar", "pe")
                s.add("scalar", lambda e, tc=tc:
                      e.activation(cbg[:, tc, 8:16], pb[6][:, 0:8], AF.Tanh, scale=0.5), inc="ac")
                s.wait("vector", "ac")
                s.add("vector", lambda e, tc=tc:
                      e.tensor_scalar(cbg[:, tc, 8:16], cbg[:, tc, 8:16], 0.5, 0.5,
                                      ALU.mult, ALU.add))
                s.add("vector", lambda e, tc=tc:
                      e.tensor_tensor(cbg[:, tc, 16:24], pb[6][:, 8:16], dtb_bc[:], ALU.add), inc="dv")
                s.bank_release(6, "dv")

            for tc in range(p2_chunks):
                ci = ti * CPT + tc
                if g_skip:
                    continue
                s.wait("scalar", "dv")
                s.add("scalar", lambda e, tc=tc:
                      e.activation(sq0[:, 0:8], cbg[:, tc, 16:24], AF.Abs), inc="ac")
                s.wait("scalar", "ac")
                s.add("scalar", lambda e:
                      e.activation(sq0[:, 8:16], sq0[:, 0:8], AF.Exp, scale=-1.0), inc="ac")
                s.wait("scalar", "ac")
                s.add("scalar", lambda e:
                      e.activation(sq0[:, 24:32], sq0[:, 8:16], AF.Ln, bias=one1[:]), inc="ac")
                s.wait("vector", "ac")
                s.add("vector", lambda e, tc=tc:
                      e.tensor_scalar_max(sq0[:, 16:24], cbg[:, tc, 16:24], 0.0), inc="dv")
                s.wait("vector", "dv")
                s.add("vector", lambda e:
                      e.tensor_tensor(sq0[:, 32:40], sq0[:, 16:24], sq0[:, 24:32], ALU.add), inc="dv")
                s.wait("vector", "dv")
                s.add("vector", lambda e, tc=tc:
                      e.tensor_tensor(cbg[:, tc, 16:24], sq0[:, 32:40], nega_bc[:], ALU.mult), inc="dv")
                s.pe_acquire(6)
                s.wait("tensor", "dv")
                s.wait("tensor", "dx")
                first = (ci == 0)
                s.add("tensor", lambda e, tc=tc, first=first:
                      e.matmul(pb[6][:, 16:24], triu_f[:], cbg[:, tc, 16:24],
                               start=True, stop=first),
                      inc="pe" if first else None)
                if not first:
                    s.add("tensor", lambda e:
                          e.matmul(pb[6][:, 16:24], onesr_f[:],
                                   cb_rows[0:1, 127:1024:128],
                                   start=False, stop=True), inc="pe")
                s.wait("vector", "pe")
                s.add("vector", lambda e, tc=tc:
                      e.tensor_copy(cbg[:, tc, 0:8], pb[6][:, 16:24]), inc="dv")
                s.bank_release(6, "dv")
                s.pe_acquire(6)
                s.wait("tensor", "dv")
                s.add("tensor", lambda e, tc=tc:
                      e.transpose(pb[6][0:16, 128:256], cbg[:, tc, 0:16], eye_f[:]), inc="pe")
                s.wait("scalar", "pe")
                s.add("scalar", lambda e:
                      e.activation(cb_fm[:], pb[6][0:16, 128:256], AF.Copy), inc="ac")
                s.bank_release(6, "ac")
                s.wait("sync", "ac")
                h1 = s.dma("sync", lambda e: e.dma_start(
                    scr_cb.ap().rearrange("o (r c) -> (o r) c", r=16), cb_fm[:]), "dx")
                s.wait("sync", h1[0], h1[1])
                h2_ = s.dma("sync", lambda e: e.dma_start(cb_rows[:], scr_cb.ap()), "dx")
                s.wait("vector", h2_[0], h2_[1])
                s.wait("scalar", h2_[0], h2_[1])
                s.wait("tensor", h2_[0], h2_[1])
                s.add("vector", lambda e:
                      e.tensor_scalar_mul(rowt[0:1, 0:512], cb_rows[0:1, 1024:1536], -1.0))
                s.add("vector", lambda e:
                      e.tensor_scalar_mul(rows2[0:1, 0:512], cb_rows[0:1, 1536:2048], -1.0), inc="dv")
                s.wait("scalar", "dv")
                s.add("scalar", lambda e: e.activation(bneg_rb[0:1, 0:512], rowt[0:1, 0:512], AF.Copy))
                s.add("scalar", lambda e: e.activation(bneg_rb[0:1, 512:1024], rows2[0:1, 0:512], AF.Copy), inc="ac")
                s.wait("tensor", "ac")
                l2norm_tensor(QB, tc)
                l2norm_tensor(KB, tc)
                otm = OTM[:]
                if p2_heads:
                    for t_ in pre_head(tc, 0, 0):
                        t_()
                for h in range(p2_heads):
                    nxt = pre_head(tc, h + 1, (h + 1) % 2) if h + 1 < p2_heads else []
                    emit_head(tc, ci, h, otm, h % 2, nxt)
                if p2_opost:
                    o_post(tc, otm)

            for si in range(2 if p2_ow else 0):
                j0 = si * 512
                slot = next_slot()
                dwv = wslab_dma(slot, lambda e, j0=j0, slot=slot:
                                e.dma_start(WRS[slot][:],
                                            ow.ap()[:, j0:j0 + 512].rearrange("(dc p) j -> p dc j", p=128)))
                s.wait("tensor", dwv[0], dwv[1])
                for jj in range(4):
                    jc = si * 4 + jj
                    bank = rot()
                    s.pe_acquire(bank)
                    for dc in range(DC):
                        s.add("tensor", lambda e, dc=dc, jj=jj, slot=slot, bank=bank:
                              e.matmul(pb[bank][:, 0:512],
                                       WRS[slot][:, dc, jj * 128:(jj + 1) * 128],
                                       OFM[:, dc, :], start=(dc == 0), stop=(dc == DC - 1)),
                              inc="pe" if dc == DC - 1 else None)
                    wslab_state[slot] = s.cnt["pe"]
                    s.wait("vector", "pe", s.cnt["pe"])
                    s.add("vector", lambda e, jc=jc, t0=t0, bank=bank:
                          e.scalar_tensor_tensor(ACC[:, jc, t0:t0 + TILE], pb[bank][:, 0:512], 0.5,
                                                 ACC[:, jc, t0:t0 + TILE], ALU.mult, ALU.add), inc="dv")
                    s.bank_release(bank, "dv")
            s.barrier()

    attn_phase()
    if upto == "p2":
        return finish(nc, s, y_h, ACC, dumps, locals())

    fr_state = {}
    FRS = (FR0, FR1)
    fr_slot = [0]
    hes_state = {}

    w2_state = {}

    def ffn_quarter(in_tile, w1_ap, w2_ap, act, b1_fn, use_we, out_scale, b2_on,
                    w2buf=None, w2_preloaded=None, w2_next=None):
        if w2buf is None:
            w2buf = W2Q
        if w2_preloaded is None:
            s.wait("gpsimd", "pe")
            w2v = s.dma("gpsimd", lambda e, w2_ap=w2_ap, w2buf=w2buf: e.dma_start(
                w2buf[:], w2_ap.rearrange("(kc p) j -> p kc j", p=128)), "dw")
        else:
            w2v = w2_preloaded
        for si in range(4):
            j0 = si * 256
            slot = fr_slot[0]
            fr_slot[0] ^= 1
            prev = fr_state.get(slot)
            if prev is not None:
                s.wait("gpsimd", "pe", prev)
            dwv = s.dma("gpsimd", lambda e, j0=j0, slot=slot, w1_ap=w1_ap:
                        e.dma_start(FRS[slot][:],
                                    w1_ap[:, j0:j0 + 256].rearrange("(dc p) j -> p dc j", p=128)), "dw")
            s.wait("tensor", dwv[0], dwv[1])
            for jj in range(2):
                jc4 = si * 2 + jj
                for tt in range(NTILE):
                    a = tt * 512
                    bank = rot()
                    s.pe_acquire(bank)
                    for dc in range(DC):
                        s.add("tensor", lambda e, dc=dc, jj=jj, a=a, slot=slot, bank=bank, in_tile=in_tile:
                              e.matmul(pb[bank][:, 0:512],
                                       FRS[slot][:, dc, jj * 128:(jj + 1) * 128],
                                       in_tile[:, dc, a:a + 512],
                                       start=(dc == 0), stop=(dc == DC - 1)),
                              inc="pe" if dc == DC - 1 else None)
                    fr_state[slot] = s.cnt["pe"]
                    s.wait("scalar", "pe", s.cnt["pe"])
                    kw_ = {}
                    if b1_fn is not None:
                        kw_ = dict(bias=b1_fn(jc4))
                    if use_we:
                        hslot = (jc4 * NTILE + tt) % 2
                        hes = HES0 if hslot == 0 else HES1
                        hprev = hes_state.get(hslot)
                        if hprev is not None:
                            s.wait("scalar", "gp", hprev)
                        s.add("scalar", lambda e, hes=hes, bank=bank, kw_=kw_:
                              e.activation(hes[:], pb[bank][:, 0:512], act, **kw_), inc="ac")
                        s.bank_release(bank, "ac")
                        s.wait("gpsimd", "ac")
                        hes_state[hslot] = s.add(
                            "gpsimd", lambda e, jc4=jc4, a=a, hes=hes:
                            e.tensor_tensor(HFC[:, jc4, a:a + 512], hes[:],
                                            WEBC[:, a:a + 512], ALU.mult), inc="gp")
                    else:
                        s.add("scalar", lambda e, jc4=jc4, a=a, bank=bank, kw_=kw_:
                              e.activation(HFC[:, jc4, a:a + 512], pb[bank][:, 0:512], act, **kw_),
                              inc="ac")
                        s.bank_release(bank, "ac")
        if w2_next is not None:
            nbuf, nap = w2_next
            prevpe = w2_state.get(id(nbuf))
            if prevpe is not None:
                s.wait("gpsimd", "pe", prevpe)
            w2_state["pending"] = (nbuf, s.dma(
                "gpsimd", lambda e, nap=nap, nbuf=nbuf: e.dma_start(
                    nbuf[:], nap.rearrange("(kc p) j -> p kc j", p=128)), "dw"))
        s.wait("tensor", w2v[0], w2v[1])
        if use_we:
            s.wait("tensor", "gp")
        else:
            s.wait("tensor", "ac")
        for tt in range(NTILE):
            a = tt * 512
            for oc in range(DC):
                bank = rot()
                s.pe_acquire(bank)
                for kc in range(8):
                    last = (kc == 7 and not b2_on)
                    s.add("tensor", lambda e, kc=kc, oc=oc, a=a, bank=bank, w2buf=w2buf:
                          e.matmul(pb[bank][:, 0:512], w2buf[:, kc, oc * 128:(oc + 1) * 128],
                                   HFC[:, kc, a:a + 512], start=(kc == 0),
                                   stop=(kc == 7 and not b2_on)),
                          inc="pe" if last else None)
                if b2_on:
                    s.add("tensor", lambda e, oc=oc, a=a, bank=bank:
                          e.matmul(pb[bank][:, 0:512], b2row[0:1, oc * 128:(oc + 1) * 128],
                                   werow[0:1, a:a + 512], start=False, stop=True), inc="pe")
                s.wait("vector", "pe", s.cnt["pe"])
                if out_scale is None:
                    s.add("vector", lambda e, oc=oc, a=a, bank=bank:
                          e.tensor_tensor(ACC[:, oc, a:a + 512], pb[bank][:, 0:512],
                                          ACC[:, oc, a:a + 512], ALU.add), inc="dv")
                else:
                    s.add("vector", lambda e, oc=oc, a=a, bank=bank, osc=out_scale:
                          e.scalar_tensor_tensor(ACC[:, oc, a:a + 512], pb[bank][:, 0:512], osc,
                                                 ACC[:, oc, a:a + 512], ALU.mult, ALU.add), inc="dv")
                s.bank_release(bank, "dv")
        w2_state[id(w2buf)] = s.cnt["pe"]

    def p3_phase():
        for nt0 in range(0, T, 512):
            a, b = nt0, nt0 + 512
            pe_sq = {}
            s.pe_acquire(2)
            s.wait("scalar", "dv")
            for dc in range(DC):
                sqb = sq0 if dc % 2 == 0 else sq1
                if dc >= 2:
                    s.wait("scalar", "pe", pe_sq[dc - 2])
                s.add("scalar", lambda e, dc=dc, a=a, b=b, sqb=sqb:
                      e.activation(sqb[:], ACC[:, dc, a:b], AF.Square), inc="ac")
                s.wait("tensor", "ac")
                pe_sq[dc] = s.add("tensor", lambda e, dc=dc, sqb=sqb:
                                  e.matmul(pb[2][0:1, 0:512], ones1[:], sqb[:],
                                           start=(dc == 0), stop=(dc == DC - 1)), inc="pe")
            s.wait("vector", "pe")
            s.add("vector", lambda e:
                  e.tensor_scalar(rowt[0:1, 0:512], pb[2][0:1, 0:512], 1.0 / D, None,
                                  ALU.mult, ALU.bypass), inc="dv")
            s.bank_release(2, "dv")
            s.wait("scalar", "dv")
            s.add("scalar", lambda e:
                  e.activation(rowt[0:1, 0:512], rowt[0:1, 0:512], AF.Ln, bias=eps6[:]))
            s.add("scalar", lambda e:
                  e.activation(rowt[0:1, 0:512], rowt[0:1, 0:512], AF.Exp, scale=-0.5), inc="ac")
            s.pe_acquire(4)
            s.wait("tensor", "ac")
            s.add("tensor", lambda e:
                  e.matmul(pb[4][:, 0:512], onesr_f[:], rowt[0:1, 0:512], start=True, stop=True), inc="pe")
            s.wait("vector", "pe")
            for dc in range(DC):
                s.add("vector", lambda e, dc=dc, a=a, b=b:
                      e.tensor_tensor(sq1[:], ACC[:, dc, a:b], pb[4][:, 0:512], ALU.mult))
                s.add("vector", lambda e, dc=dc, a=a, b=b:
                      e.tensor_scalar_mul(H2[:, dc, a:b], sq1[:], ln2c[:, dc:dc + 1]),
                      inc="dv" if dc == DC - 1 else None)
            s.bank_release(4, "dv")
            s.wait("tensor", "dv")
        s.barrier()
        for q in range(4):
            ffn_quarter(H2, fcw.ap()[:, q * 1024:(q + 1) * 1024],
                        pw.ap()[q * 1024:(q + 1) * 1024, :], AF.Silu,
                        None, False, 0.5, False)
            s.barrier()

    p3_phase()
    if upto == "p3":
        return finish(nc, s, y_h, ACC, dumps, locals())

    def router_phase():
        for tcg in range(p4_tcg):
            s.pe_acquire(6)
            for dc in range(DC):
                s.add("tensor", lambda e, dc=dc, tcg=tcg:
                      e.matmul(pb[6][:, 0:12], XN[:, dc, tcg * 128:(tcg + 1) * 128],
                               rlw[:, dc, :], start=(dc == 0), stop=(dc == DC - 1)),
                      inc="pe" if dc == DC - 1 else None)
            s.wait("vector", "pe")
            s.add("vector", lambda e: e.tensor_tensor(sq0[:, 0:12], pb[6][:, 0:12], rlb_bc[:], ALU.add), inc="dv")
            s.bank_release(6, "dv")
            s.wait("vector", "dv")
            s.add("vector", lambda e: e.reduce_max(l2s[:, 0:1], sq0[:, 8:12], AX.X), inc="dv")
            s.wait("vector", "dv")
            s.add("vector", lambda e:
                  e.tensor_scalar(sq0[:, 16:20], sq0[:, 8:12], l2s[:, 0:1], None,
                                  ALU.subtract, ALU.bypass), inc="dv")
            s.wait("scalar", "dv")
            s.add("scalar", lambda e: e.activation(sq0[:, 16:20], sq0[:, 16:20], AF.Exp), inc="ac")
            s.wait("vector", "ac")
            s.add("vector", lambda e: e.reduce_sum(l2s[:, 1:2], sq0[:, 16:20], AX.X), inc="dv")
            s.wait("vector", "dv")
            s.add("vector", lambda e: e.reciprocal(l2s[:, 2:3], l2s[:, 1:2]), inc="dv")
            s.wait("vector", "dv")
            s.add("vector", lambda e:
                  e.tensor_scalar_mul(sq0[:, 20:24], sq0[:, 16:20], l2s[:, 2:3]), inc="dv")
            s.wait("vector", "dv")
            s.add("vector", lambda e: e.tensor_tensor(sq0[:, 24:27], sq0[:, 20:23], sq0[:, 21:24], ALU.add))
            s.add("vector", lambda e: e.tensor_copy(sq0[:, 27:28], sq0[:, 23:24]), inc="dv")
            s.wait("vector", "dv")
            s.add("vector", lambda e:
                  e.scalar_tensor_tensor(sq0[:, 32:36], sq0[:, 24:28], 0.1, sq0[:, 0:4],
                                         ALU.mult, ALU.add))
            s.add("vector", lambda e:
                  e.scalar_tensor_tensor(sq0[:, 36:40], sq0[:, 24:28], 0.1, sq0[:, 4:8],
                                         ALU.mult, ALU.add), inc="dv")
            s.wait("vector", "dv")
            s.add("vector", lambda e: e.reduce_max(l2s[:, 3:4], sq0[:, 32:40], AX.X), inc="dv")
            s.wait("vector", "dv")
            s.add("vector", lambda e:
                  e.tensor_scalar(sq0[:, 32:40], sq0[:, 32:40], l2s[:, 3:4], None,
                                  ALU.subtract, ALU.bypass), inc="dv")
            s.wait("scalar", "dv")
            s.add("scalar", lambda e: e.activation(sq0[:, 32:40], sq0[:, 32:40], AF.Exp), inc="ac")
            s.wait("vector", "ac")
            s.add("vector", lambda e: e.reduce_sum(l2s[:, 4:5], sq0[:, 32:40], AX.X), inc="dv")
            s.wait("vector", "dv")
            s.add("vector", lambda e: e.reciprocal(l2s[:, 5:6], l2s[:, 4:5]), inc="dv")
            s.wait("vector", "dv")
            s.add("vector", lambda e:
                  e.tensor_scalar_mul(sq0[:, 40:48], sq0[:, 32:40], l2s[:, 5:6]), inc="dv")
            s.wait("vector", "dv")
            s.add("vector", lambda e: e.reduce_max(l2s[:, 6:7], sq0[:, 40:48], AX.X), inc="dv")
            s.wait("vector", "dv")
            s.add("vector", lambda e:
                  e.tensor_scalar(sq0[:, 48:56], sq0[:, 40:48], l2s[:, 6:7], None,
                                  ALU.is_ge, ALU.bypass), inc="dv")
            s.wait("vector", "dv")
            s.add("vector", lambda e:
                  e.scalar_tensor_tensor(sq0[:, 56:64], sq0[:, 48:56], NEG, sq0[:, 40:48],
                                         ALU.mult, ALU.add), inc="dv")
            s.wait("vector", "dv")
            s.add("vector", lambda e: e.reduce_max(l2s[:, 7:8], sq0[:, 56:64], AX.X), inc="dv")
            s.wait("vector", "dv")
            s.add("vector", lambda e:
                  e.tensor_scalar(sq0[:, 48:56], sq0[:, 40:48], l2s[:, 7:8], None,
                                  ALU.is_ge, ALU.bypass), inc="dv")
            s.wait("vector", "dv")
            s.add("vector", lambda e:
                  e.tensor_tensor(sq0[:, 56:64], sq0[:, 40:48], sq0[:, 48:56], ALU.mult), inc="dv")
            s.wait("vector", "dv")
            s.add("vector", lambda e: e.reduce_sum(l2s[:, 0:1], sq0[:, 56:64], AX.X), inc="dv")
            s.wait("vector", "dv")
            s.add("vector", lambda e: e.reciprocal(l2s[:, 1:2], l2s[:, 0:1]), inc="dv")
            s.wait("vector", "dv")
            s.add("vector", lambda e, tcg=tcg:
                  e.tensor_scalar_mul(wetm[:, tcg, :], sq0[:, 56:64], l2s[:, 1:2]), inc="dv")
            s.pe_acquire(6)
            s.wait("tensor", "dv")
            s.add("tensor", lambda e, tcg=tcg:
                  e.transpose(pb[6][0:8, 128:256], wetm[:, tcg, :], eye_f[:]), inc="pe")
            s.wait("scalar", "pe")
            s.add("scalar", lambda e, tcg=tcg:
                  e.activation(wefm[:, tcg * 128:(tcg + 1) * 128], pb[6][0:8, 128:256], AF.Copy), inc="ac")
            s.bank_release(6, "ac")
        s.wait("sync", "ac")
        s.dma("sync", lambda e: e.dma_start(scr_we.ap(), wefm[:]), "dx")
        s.barrier()

    def we_emit(ei):
        wv = s.dma("sync", lambda e, ei=ei: e.dma_start(werow[:], scr_we.ap()[ei:ei + 1, :]), "dx")
        for ntt in range(4):
            bank = rot()
            s.pe_acquire(bank)
            s.wait("tensor", wv[0], wv[1])
            s.add("tensor", lambda e, ntt=ntt, bank=bank:
                  e.matmul(pb[bank][:, 0:512], onesr_b[:], werow[0:1, ntt * 512:(ntt + 1) * 512],
                           start=True, stop=True), inc="pe")
            s.wait("scalar", "pe")
            s.add("scalar", lambda e, ntt=ntt, bank=bank:
                  e.activation(WEBC[:, ntt * 512:(ntt + 1) * 512], pb[bank][:, 0:512], AF.Copy), inc="ac")
            s.bank_release(bank, "ac")
        bv1 = s.dma("gpsimd", lambda e, ei=ei: e.dma_start(b1col[:], b1t.ap()[ei]), "dw")
        bv2 = s.dma("gpsimd", lambda e, ei=ei: e.dma_start(b2row[:], b2.ap()[ei:ei + 1, :]), "dw")
        s.wait("scalar", bv1[0], bv1[1])
        s.wait("tensor", bv2[0], bv2[1])
        s.wait("gpsimd", "ac")

    router_phase()
    w2bufs = (W2Q, W2Q1)
    qlist = [(ei, q) for ei in range(p4_experts) for q in range(4)]
    for qi, (ei, q) in enumerate(qlist):
        if q == 0:
            we_emit(ei)
        w2ap = w2.ap()[ei][q * 1024:(q + 1) * 1024, :]
        pend = w2_state.pop("pending", None)
        w2buf = w2bufs[qi % 2]
        pre = None
        if pend is not None and pend[0] is w2buf:
            pre = pend[1]
        nxt = None
        if qi + 1 < len(qlist):
            nei, nq = qlist[qi + 1]
            nxt = (w2bufs[(qi + 1) % 2], w2.ap()[nei][nq * 1024:(nq + 1) * 1024, :])
        ffn_quarter(XN, w1.ap()[ei][:, q * 1024:(q + 1) * 1024], w2ap, AF.Gelu,
                    (lambda jc4, q=q: b1col[:, q * 8 + jc4:q * 8 + jc4 + 1]),
                    True, None, q == 0, w2buf=w2buf, w2_preloaded=pre, w2_next=nxt)
        if q == 3 or qi == len(qlist) - 1:
            s.barrier()
        else:
            s.barrier()

    if upto == "p4":
        return finish(nc, s, y_h, ACC, dumps, locals())

    dv_slot = {}
    for dc in range(DC):
        slot = dc % 2
        prev = dv_slot.get(slot)
        if prev is not None:
            s.wait("sync", "dv", prev)
        xv = s.dma("sync", lambda e, dc=dc, slot=slot:
                   e.dma_start(XR[:, slot, :], xT.ap()[dc * 128:(dc + 1) * 128, :]), "dx")
        s.wait("vector", xv[0], xv[1])
        dv_slot[slot] = s.add("vector", lambda e, dc=dc, slot=slot:
                              e.tensor_tensor(ACC[:, dc, :], ACC[:, dc, :], XR[:, slot, :],
                                              ALU.add), inc="dv")

    return finish(nc, s, y_h, ACC, dumps, locals())


def finish(nc, s, y_h, ACC, dumps, env):
    s.barrier()
    hs = []
    for dc in range(DC):
        hs.append(s.dma("sync", lambda e, dc=dc:
                        e.dma_start(y_h.ap()[dc * 128:(dc + 1) * 128, :], ACC[:, dc, :]), "dx"))
    for name in dumps:
        tile = env[name]
        sh = list(tile.shape)
        flat = [sh[0], int(np.prod(sh[1:]))]
        dh = nc.dram_tensor("dump_" + name, flat, F32, kind="ExternalOutput")
        ap = tile[:]
        if len(sh) == 3:
            ap = ap.rearrange("p a b -> p (a b)")
        if tile.dtype == F32:
            hs.append(s.dma("sync", lambda e, dh=dh, ap=ap: e.dma_start(dh.ap(), ap), "dx"))
        else:
            DUMPF = env["DUMPF"]
            for hh in hs:
                s.wait("scalar", hh[0], hh[1])
            s.add("scalar", lambda e, ap=ap, w=flat[1], DUMPF=DUMPF:
                  e.activation(DUMPF[:, 0:w], ap, AF.Copy), inc="ac")
            s.wait("sync", "ac")
            hs.append(s.dma("sync", lambda e, dh=dh, w=flat[1], DUMPF=DUMPF:
                            e.dma_start(dh.ap(), DUMPF[:, 0:w]), "dx"))
    for fam in list(s.dma_fams):
        s.wait_fam("sync", fam)

    with nc.Block() as block:
        @block.tensor
        def _(e):
            for f in s.q["tensor"]:
                f(e)

        @block.vector
        def _(e):
            for f in s.q["vector"]:
                f(e)

        @block.scalar
        def _(e):
            for f in s.q["scalar"]:
                f(e)

        @block.gpsimd
        def _(e):
            for f in s.q["gpsimd"]:
                f(e)

        @block.sync
        def _(e):
            for f in s.q["sync"]:
                f(e)

    return nc


def make_in_maps(inputs):
    x = np.asarray(inputs["L0_attn_x"], np.float32)
    B = x.shape[0]
    g = lambda k: np.ascontiguousarray(np.asarray(inputs[k], np.float32))
    b1 = np.ascontiguousarray(g("L0_ffn_b1").reshape(8, 32, 128).transpose(0, 2, 1))
    shared = dict(
        qw=g("L0_attn_qw"), kw=g("L0_attn_kw"), vw=g("L0_attn_vw"),
        gw=g("L0_attn_gw"), ow=g("L0_attn_ow"),
        bw=g("L0_attn_bw"), aw=g("L0_attn_aw"),
        fcw=g("L0_attn_fcw"), pw=g("L0_attn_pw"),
        w1=g("L0_ffn_w1"), w2=g("L0_ffn_w2"),
        b1t=b1, b2=g("L0_ffn_b2"),
        ln1c=np.ascontiguousarray(g("L0_attn_ln1_w").reshape(8, 128).T),
        ln2c=np.ascontiguousarray(g("L0_attn_ln2_w").reshape(8, 128).T),
        onorm_r=g("L0_attn_onorm_w").reshape(1, 128),
        alog_r=g("L0_attn_A_log").reshape(1, 8),
        dtb_r=g("L0_attn_dt_bias").reshape(1, 8),
        rwt=np.ascontiguousarray(g("L0_ffn_router_w").T),
        lwt=np.ascontiguousarray(g("L0_ffn_lang_w").T),
        rb_r=g("L0_ffn_router_b").reshape(1, 8),
        lb_r=g("L0_ffn_lang_b").reshape(1, 4),
    )
    return [dict(shared, xT=np.ascontiguousarray(x[b].T)) for b in range(B)]


def kernel(**inputs):
    nc = build("full")
    in_maps = make_in_maps(inputs)
    res = run_bass_kernel_spmd(nc, in_maps, core_ids=list(range(len(in_maps))))
    out = np.stack([np.ascontiguousarray(np.asarray(r["y"]).T) for r in res.results], axis=0)
    return out.astype(np.float32)


# version 05
# speedup vs baseline: 1.0388x; 1.0388x over previous
import numpy as np
import concourse.bass as bass
import concourse.mybir as mybir
from concourse.bass_utils import run_bass_kernel_spmd

F32 = mybir.dt.float32
BF16 = mybir.dt.bfloat16
AF = mybir.ActivationFunctionType
ALU = mybir.AluOpType
AX = mybir.AxisListType

T, D, H, HD, DC = 2048, 1024, 8, 128, 8
C = 128
NCH = T // C
TILE = 512
NTILE = T // TILE
CPT = TILE // C
NE = 8
NEG = -1.0e30

ENGS = ("tensor", "vector", "scalar", "gpsimd", "sync")
MILESTONES = ("pe", "dv", "ac", "gp")


class Sched:
    def __init__(self, nc):
        self.nc = nc
        self.q = {e: [] for e in ENGS}
        self.sems = {}
        self.cnt = {}
        self.bank_free = {}
        self.dma_fams = {}

    def sem(self, name):
        if name not in self.sems:
            self.sems[name] = self.nc.alloc_semaphore(name=name)
            self.cnt[name] = 0
        return self.sems[name]

    def wait(self, eng, name, val=None):
        sm = self.sem(name)
        v = self.cnt[name] if val is None else val
        if v > 0:
            self.q[eng].append(lambda e, sm=sm, v=v: e.wait_ge(sm, v))

    def add(self, eng, fn, inc=None, inck=1):
        if inc is None:
            self.q[eng].append(fn)
            return None
        sm = self.sem(inc)
        self.cnt[inc] += inck

        def g(e, fn=fn, sm=sm, k=inck):
            fn(e).then_inc(sm, k)

        self.q[eng].append(g)
        return self.cnt[inc]

    def dma(self, eng, fn, fam):
        st = self.dma_fams.setdefault(fam, {"k": 8, "i": 0})
        idx = st["i"] % st["k"]
        st["i"] += 1
        name = f"{fam}{idx}"
        self.sem(name)
        cur = self.cnt[name]
        if cur > 0:
            self.wait(eng, name, cur)
        v = self.add(eng, fn, inc=name, inck=16)
        return (name, v)

    def wait_fam(self, eng, fam):
        st = self.dma_fams.get(fam)
        if not st:
            return
        for i in range(st["k"]):
            name = f"{fam}{i}"
            if name in self.cnt:
                self.wait(eng, name)

    def pe_acquire(self, bank):
        for (sname, v) in self.bank_free.get(bank, ()):
            self.wait("tensor", sname, v)
        self.bank_free[bank] = []

    def bank_release(self, bank, semname):
        self.bank_free.setdefault(bank, []).append((semname, self.cnt[semname]))

    def barrier(self):
        for e in ENGS:
            for n in MILESTONES:
                if n in self.cnt:
                    self.wait(e, n)
            for fam in self.dma_fams:
                self.wait_fam(e, fam)


class SB:
    def __init__(self, nc, base=16512, top=229344):
        self.nc, self.cur, self.top = nc, base, top

    def alloc(self, name, shape, dtype):
        per = int(np.prod(shape[1:])) * mybir.dt.size(dtype)
        off = (self.cur + 31) // 32 * 32
        t = self.nc.alloc_sbuf_tensor_at(name, list(shape), dtype, offset=off)
        self.cur = off + per
        assert self.cur <= self.top, f"SBUF overflow at {name}: {self.cur}>{self.top}"
        return t

    def mark(self):
        return self.cur

    def reset(self, m):
        self.cur = m


def build(upto="full", dumps=(), p2_tiles=NTILE, p2_chunks=CPT, p2_heads=H, p2_ow=True, p2_opost=True, headops=8, g_skip=False, g_mode=0, p4_tcg=NCH, p4_experts=NE):
    nc = bass.Bass("TRN2", target_bir_lowering=False, detect_race_conditions=False)

    def din(name, shape):
        return nc.dram_tensor(name, list(shape), F32, kind="ExternalInput")

    xT = din("xT", [D, T])
    qw, kw, vw, gw, ow = (din(n, [D, D]) for n in ("qw", "kw", "vw", "gw", "ow"))
    bw, aw = din("bw", [D, H]), din("aw", [D, H])
    fcw, pw = din("fcw", [D, 4 * D]), din("pw", [4 * D, D])
    w1, w2 = din("w1", [NE, D, 4 * D]), din("w2", [NE, 4 * D, D])
    b1t, b2 = din("b1t", [NE, 128, 32]), din("b2", [NE, D])
    ln1c_h, ln2c_h = din("ln1c", [128, 8]), din("ln2c", [128, 8])
    onorm_r = din("onorm_r", [1, HD])
    alog_r, dtb_r = din("alog_r", [1, H]), din("dtb_r", [1, H])
    rwt, lwt = din("rwt", [D, NE]), din("lwt", [D, 4])
    rb_r, lb_r = din("rb_r", [1, NE]), din("lb_r", [1, 4])
    y_h = nc.dram_tensor("y", [D, T], F32, kind="ExternalOutput")
    scr_cb = nc.dram_tensor("scr_cb", [1, 2048], F32, kind="Internal")
    scr_we = nc.dram_tensor("scr_we", [NE, T], BF16, kind="Internal")

    eye_d = nc.inline_tensor(np.eye(128, dtype=np.float32), "eye_d")
    triu_d = nc.inline_tensor(np.triu(np.ones((128, 128), np.float32)), "triu_d")
    ii, jj = np.indices((128, 128))
    mus_d = nc.inline_tensor(np.where(jj > ii, 0.0, NEG).astype(np.float32), "mus_d")
    mls_d = nc.inline_tensor(np.where(jj < ii, 0.0, NEG).astype(np.float32), "mls_d")
    mui_d = nc.inline_tensor(np.where(jj >= ii, 0.0, NEG).astype(np.float32), "mui_d")

    pb = [nc.alloc_psum_tensor(f"pb{i}", [128, 512], F32) for i in range(7)]
    pt = nc.alloc_psum_tensor("pt", [128, 1024], BF16)

    sba = SB(nc)
    ACC = sba.alloc("ACC", [128, DC, T], F32)
    XN = sba.alloc("XN", [128, DC, T], BF16)
    eye_f = sba.alloc("eye_f", [128, 128], F32)
    eye_b = sba.alloc("eye_b", [128, 128], BF16)
    triu_f = sba.alloc("triu_f", [128, 128], F32)
    musf = sba.alloc("musf", [128, 128], F32)
    mlsf = sba.alloc("mlsf", [128, 128], F32)
    muif = sba.alloc("muif", [128, 128], F32)
    ones1 = sba.alloc("ones1", [128, 1], F32)
    onesr_f = sba.alloc("onesr_f", [1, 128], F32)
    onesr_b = sba.alloc("onesr_b", [1, 128], BF16)
    epsln = sba.alloc("epsln", [1, 1], F32)
    eps6 = sba.alloc("eps6", [1, 1], F32)
    eps6c = sba.alloc("eps6c", [128, 1], F32)
    one1 = sba.alloc("one1", [128, 1], F32)
    ln1c = sba.alloc("ln1c", [128, 8], F32)
    ln2c = sba.alloc("ln2c", [128, 8], F32)
    onorm_bc = sba.alloc("onorm_bc", [128, 128], F32)
    dtb_bc = sba.alloc("dtb_bc", [128, 8], F32)
    nega_bc = sba.alloc("nega_bc", [128, 8], F32)
    rlw = sba.alloc("rlw", [128, DC, 12], BF16)
    rlb_bc = sba.alloc("rlb_bc", [128, 12], F32)
    arow = sba.alloc("arow", [1, 16], F32)
    l2s = sba.alloc("l2s", [128, 8], F32)
    rowsc = sba.alloc("rowsc", [1, 512], F32)
    rows2 = sba.alloc("rows2", [1, 512], F32)
    rowt = sba.alloc("rowt", [1, 512], F32)
    sq0 = sba.alloc("sq0", [128, 512], F32)
    sq1 = sba.alloc("sq1", [128, 512], F32)
    l2q = sba.alloc("l2q", [128, 1024], F32)

    ph = sba.mark()
    DUMPF = None
    ph2 = sba.mark()
    XF = sba.alloc("XF", [128, DC, T], F32)
    sba.reset(ph2)
    QB = sba.alloc("QB", [128, CPT, 1024], BF16)
    KB = sba.alloc("KB", [128, CPT, 1024], BF16)
    VB = sba.alloc("VB", [128, CPT, 1024], BF16)
    GB = sba.alloc("GB", [128, CPT, 1024], BF16)
    HB = sba.alloc("HB", [128, DC, TILE], BF16)
    OTM = sba.alloc("OTM", [128, 1024], F32)
    OFM = sba.alloc("OFM", [128, DC, TILE], BF16)
    WR0 = sba.alloc("WR0", [128, DC, 512], BF16)
    WR1 = sba.alloc("WR1", [128, DC, 512], BF16)
    BAW = sba.alloc("BAW", [128, DC, 16], BF16)
    S_f = sba.alloc("S_f", [128, H, HD], F32)
    S_b = sba.alloc("S_b", [128, H, HD], BF16)
    cbg = sba.alloc("cbg", [128, CPT, 24], F32)
    cb_fm = sba.alloc("cb_fm", [16, 128], F32)
    cb_rows = sba.alloc("cb_rows", [1, 2048], F32)
    bneg_rb = sba.alloc("bneg_rb", [1, 1024], BF16)
    qfm = sba.alloc("qfm", [128, 128], BF16)
    kfm = sba.alloc("kfm", [128, 128], BF16)
    kdec = sba.alloc("kdec", [128, 128], BF16)
    bm = sba.alloc("bm", [128, 128], BF16)
    bmt = sba.alloc("bmt", [128, 128], BF16)
    mqt = sba.alloc("mqt", [128, 128], BF16)
    u0 = sba.alloc("u0", [128, 128], BF16)
    u1 = sba.alloc("u1", [128, 128], BF16)
    pp = [sba.alloc(f"pp{i}", [128, 128], BF16) for i in range(2)]
    ppt = [sba.alloc(f"ppt{i}", [128, 128], BF16) for i in range(2)]
    negd = sba.alloc("negd", [128, 128], F32)
    dml = sba.alloc("dml", [128, 128], F32)
    dmus = sba.alloc("dmus", [128, 128], F32)
    dmui = sba.alloc("dmui", [128, 128], F32)
    x1s = sba.alloc("x1s", [128, 128], F32)
    gamc = sba.alloc("gamc", [128, 1], F32)
    gamCc = sba.alloc("gamCc", [128, 1], F32)
    decc = sba.alloc("decc", [128, 1], F32)
    osb = sba.alloc("osb", [128, 128], F32)
    sba.reset(ph2)
    H2 = sba.alloc("H2", [128, DC, T], BF16)
    p34 = sba.mark()
    HFC = sba.alloc("HFC", [128, 8, T], BF16)
    W2Q = sba.alloc("W2Q", [128, 8, 1024], BF16)
    FR0 = sba.alloc("FR0", [128, DC, 256], BF16)
    FR1 = sba.alloc("FR1", [128, DC, 256], BF16)
    sba.reset(ph2)
    WEBC = sba.alloc("WEBC", [128, T], BF16)
    HES0 = sba.alloc("HES0", [128, 512], F32)
    HES1 = sba.alloc("HES1", [128, 512], F32)
    werow = sba.alloc("werow", [1, 2048], BF16)
    b2row = sba.alloc("b2row", [1, 1024], BF16)
    b1col = sba.alloc("b1col", [128, 32], F32)
    p4r = sba.mark()
    wetm = sba.alloc("wetm", [128, NCH, 8], F32)
    wefm = sba.alloc("wefm", [8, T], BF16)
    sba.reset(p4r)
    W2Q1 = sba.alloc("W2Q1", [128, 8, 1024], BF16)
    assert sba.mark() <= p34, "P4 overlay exceeds H2 space"
    sba.reset(ph2)
    XR = sba.alloc("XR", [128, 2, T], F32)
    sba.reset(ph)

    s = Sched(nc)
    for n in MILESTONES:
        s.sem(n)
    bank_rot = [0]

    def rot():
        b = bank_rot[0]
        bank_rot[0] = (bank_rot[0] + 1) % 4
        return b

    for dst, src in ((eye_f, eye_d), (triu_f, triu_d), (musf, mus_d),
                     (mlsf, mls_d), (muif, mui_d)):
        s.dma("sync", lambda e, dst=dst, src=src: e.dma_start(dst[:], src.ap()), "dx")
    s.dma("sync", lambda e: e.dma_start(ln1c[:], ln1c_h.ap()), "dx")
    s.dma("sync", lambda e: e.dma_start(ln2c[:], ln2c_h.ap()), "dx")
    s.dma("sync", lambda e: e.dma_start(arow[0:1, 0:8], alog_r.ap()), "dx")
    s.dma("sync", lambda e: e.dma_start(arow[0:1, 8:16], dtb_r.ap()), "dx")
    s.dma("sync", lambda e: e.dma_start(rlb_bc[0:1, 0:8], rb_r.ap()), "dx")
    s.dma("sync", lambda e: e.dma_start(rlb_bc[0:1, 8:12], lb_r.ap()), "dx")
    s.dma("sync", lambda e: e.dma_start(rowt[0:1, 0:128], onorm_r.ap()), "dx")
    s.dma("gpsimd", lambda e: e.dma_start(
        rlw[:, :, 0:8], rwt.ap().rearrange("(dc p) e -> p dc e", p=128)), "dw")
    s.dma("gpsimd", lambda e: e.dma_start(
        rlw[:, :, 8:12], lwt.ap().rearrange("(dc p) e -> p dc e", p=128)), "dw")
    for tile_, v_ in ((ones1, 1.0), (onesr_f, 1.0), (onesr_b, 1.0), (epsln, 1e-5),
                      (eps6, 1e-6), (eps6c, 1e-6), (one1, 1.0)):
        s.add("gpsimd", lambda e, t=tile_, v=v_: e.memset(t[:], v),
              inc="gp" if tile_ is one1 else None)
    s.wait_fam("scalar", "dx")
    s.wait_fam("scalar", "dw")
    s.wait("scalar", "gp")
    s.add("scalar", lambda e: e.activation(eye_b[:], eye_f[:], AF.Copy), inc="ac")
    s.add("scalar", lambda e: e.activation(arow[0:1, 0:8], arow[0:1, 0:8], AF.Exp), inc="ac")
    s.wait("vector", "ac")
    s.add("vector", lambda e: e.tensor_scalar_mul(arow[0:1, 0:8], arow[0:1, 0:8], -1.0), inc="dv")
    s.wait("tensor", "dv")
    s.wait_fam("tensor", "dx")
    s.wait_fam("tensor", "dw")
    s.add("tensor", lambda e: e.matmul(pb[4][:, 0:16], onesr_f[:], arow[:], start=True, stop=True), inc="pe")
    s.add("tensor", lambda e: e.matmul(pb[5][:, 0:128], onesr_f[:], rowt[0:1, 0:128], start=True, stop=True), inc="pe")
    s.add("tensor", lambda e: e.matmul(pb[6][:, 0:12], onesr_f[:], rlb_bc[0:1, 0:12], start=True, stop=True), inc="pe")
    s.wait("vector", "pe")
    s.add("vector", lambda e: e.tensor_copy(nega_bc[:], pb[4][:, 0:8]))
    s.add("vector", lambda e: e.tensor_copy(dtb_bc[:], pb[4][:, 8:16]))
    s.add("vector", lambda e: e.tensor_copy(onorm_bc[:], pb[5][:, 0:128]))
    s.add("vector", lambda e: e.tensor_copy(rlb_bc[:], pb[6][:, 0:12]), inc="dv")
    for b_ in (4, 5, 6):
        s.bank_release(b_, "dv")

    xf_ld = []
    for dc in range(DC):
        xf_ld.append(s.dma("sync", lambda e, dc=dc:
                           e.dma_start(XF[:, dc, :], xT.ap()[dc * 128:(dc + 1) * 128, :]), "dx"))

    for nt0 in range(0, T, 512):
        a, b = nt0, nt0 + 512
        pe_sq = {}
        s.pe_acquire(2)
        s.pe_acquire(3)
        s.wait("scalar", "dv")
        for dc in range(DC):
            sqb = sq0 if dc % 2 == 0 else sq1
            if dc >= 2:
                s.wait("scalar", "pe", pe_sq[dc - 2])
            s.wait("scalar", xf_ld[dc][0], xf_ld[dc][1])
            s.add("scalar", lambda e, dc=dc, a=a, b=b, sqb=sqb:
                  e.activation(sqb[:], XF[:, dc, a:b], AF.Square), inc="ac")
            s.wait("tensor", "ac")
            s.add("tensor", lambda e, dc=dc, a=a, b=b:
                  e.matmul(pb[2][0:1, 0:512], ones1[:], XF[:, dc, a:b],
                           start=(dc == 0), stop=(dc == DC - 1)))
            pe_sq[dc] = s.add("tensor", lambda e, dc=dc, sqb=sqb:
                              e.matmul(pb[3][0:1, 0:512], ones1[:], sqb[:],
                                       start=(dc == 0), stop=(dc == DC - 1)), inc="pe")
        s.wait("vector", "pe")
        s.add("vector", lambda e, a=a, b=b:
              e.tensor_scalar_mul(rowsc[0:1, 0:512], pb[2][0:1, 0:512], 1.0 / D), inc="dv")
        s.add("vector", lambda e:
              e.tensor_scalar(rowt[0:1, 0:512], pb[3][0:1, 0:512], 1.0 / D, None,
                              ALU.mult, ALU.bypass), inc="dv")
        s.bank_release(2, "dv")
        s.bank_release(3, "dv")
        s.wait("scalar", "dv")
        s.add("scalar", lambda e, a=a, b=b:
              e.activation(sq0[0:1, 0:512], rowsc[0:1, 0:512], AF.Square), inc="ac")
        s.wait("vector", "ac")
        s.add("vector", lambda e:
              e.tensor_tensor(rowt[0:1, 0:512], rowt[0:1, 0:512], sq0[0:1, 0:512],
                              ALU.subtract), inc="dv")
        s.wait("scalar", "dv")
        s.add("scalar", lambda e:
              e.activation(rowt[0:1, 0:512], rowt[0:1, 0:512], AF.Ln, bias=epsln[:]))
        s.add("scalar", lambda e, a=a, b=b:
              e.activation(rows2[0:1, 0:512], rowt[0:1, 0:512], AF.Exp, scale=-0.5), inc="ac")
        s.pe_acquire(4)
        s.pe_acquire(5)
        s.wait("tensor", "ac")
        s.add("tensor", lambda e, a=a, b=b:
              e.matmul(pb[4][:, 0:512], onesr_f[:], rowsc[0:1, 0:512], start=True, stop=True), inc="pe")
        s.add("tensor", lambda e, a=a, b=b:
              e.matmul(pb[5][:, 0:512], onesr_f[:], rows2[0:1, 0:512], start=True, stop=True), inc="pe")
        s.wait("vector", "pe")
        s.wait("scalar", "pe")
        for dc in range(DC):
            sqb = sq0 if dc % 2 == 0 else sq1
            s.add("vector", lambda e, dc=dc, a=a, b=b, sqb=sqb:
                  e.tensor_tensor(sqb[:], XF[:, dc, a:b], pb[4][:, 0:512], ALU.subtract))
            s.add("vector", lambda e, dc=dc, a=a, b=b, sqb=sqb:
                  e.tensor_tensor(ACC[:, dc, a:b], sqb[:], pb[5][:, 0:512], ALU.mult), inc="dv")
            s.wait("scalar", "dv")
            s.add("scalar", lambda e, dc=dc, a=a, b=b:
                  e.activation(XN[:, dc, a:b], ACC[:, dc, a:b], AF.Copy), inc="ac")
        s.bank_release(4, "dv")
        s.bank_release(5, "dv")
    s.barrier()

    if upto == "p1":
        return finish(nc, s, y_h, ACC, dumps, locals())

    wslab_state = {}
    WRS = (WR0, WR1)
    slot_i = [0]

    def next_slot():
        sl = slot_i[0]
        slot_i[0] ^= 1
        return sl

    def wslab_dma(slot, fn):
        prev = wslab_state.get(slot)
        if prev is not None:
            s.wait("gpsimd", "pe", prev)
        return s.dma("gpsimd", fn, "dw")

    def proj_tensor(wt_h, dst_tile, act):
        for si in range(2):
            j0 = si * 512
            slot = next_slot()
            dwv = wslab_dma(slot, lambda e, j0=j0, slot=slot, wt_h=wt_h:
                            e.dma_start(WRS[slot][:],
                                        wt_h.ap()[:, j0:j0 + 512].rearrange("(dc p) j -> p dc j", p=128)))
            s.wait("tensor", dwv[0], dwv[1])
            for tc in range(CPT):
                bank = rot()
                s.pe_acquire(bank)
                for dc in range(DC):
                    r = s.add("tensor", lambda e, dc=dc, tc=tc, slot=slot, bank=bank:
                              e.matmul(pb[bank][:, 0:512],
                                       HB[:, dc, tc * 128:(tc + 1) * 128],
                                       WRS[slot][:, dc, :],
                                       start=(dc == 0), stop=(dc == DC - 1)),
                              inc="pe" if dc == DC - 1 else None)
                wslab_state[slot] = s.cnt["pe"]
                s.wait("scalar", "pe", s.cnt["pe"])
                s.add("scalar", lambda e, tc=tc, j0=j0, bank=bank, dst_tile=dst_tile:
                      e.activation(dst_tile[:, tc, j0:j0 + 512], pb[bank][:, 0:512], act), inc="ac")
                s.bank_release(bank, "ac")

    def emit_head(tc, ci, h, otm, headops=None):
        if headops is None:
            headops = 8
        s.pe_acquire(7)
        s.add("tensor", lambda e, tc=tc, h=h:
              e.transpose(pt[:, 0:128], QB[:, tc, h * 128:(h + 1) * 128], eye_b[:]))
        s.add("tensor", lambda e, tc=tc, h=h:
              e.transpose(pt[:, 128:256], KB[:, tc, h * 128:(h + 1) * 128], eye_b[:]), inc="pe")
        s.wait("scalar", "pe")
        s.add("scalar", lambda e: e.activation(qfm[:], pt[:, 0:128], AF.Copy))
        s.add("scalar", lambda e: e.activation(kfm[:], pt[:, 128:256], AF.Copy), inc="ac")
        s.bank_release(7, "ac")
        if headops <= 1:
            return
        s.pe_acquire(2)
        s.add("tensor", lambda e, h=h:
              e.matmul(pb[2][:, 0:128], onesr_f[:], cb_rows[0:1, h * 128:(h + 1) * 128],
                       start=True, stop=True))
        s.add("tensor", lambda e, h=h:
              e.matmul(pb[2][:, 128:256], onesr_b[:], bneg_rb[0:1, h * 128:(h + 1) * 128],
                       start=True, stop=True))
        s.add("tensor", lambda e, h=h:
              e.matmul(pb[2][:, 256:257], onesr_f[:],
                       cb_rows[0:1, h * 128 + 127:h * 128 + 128], start=True, stop=True), inc="pe")
        if headops <= 2:
            s.wait("vector", "pe")
            s.add("vector", lambda e: e.tensor_copy(negd[:], pb[2][:, 0:128]), inc="dv")
            s.bank_release(2, "dv")
            return
        s.pe_acquire(3)
        s.wait("tensor", "ac")
        s.add("tensor", lambda e: e.matmul(pb[3][:, 0:128], kfm[:], kfm[:], start=True, stop=True))
        s.add("tensor", lambda e: e.matmul(pb[3][:, 128:256], kfm[:], qfm[:], start=True, stop=True), inc="pe")
        s.pe_acquire(4)
        s.add("tensor", lambda e, h=h: e.matmul(pb[4][:, 0:128], kfm[:], S_b[:, h, :], start=True, stop=True))
        s.add("tensor", lambda e, h=h: e.matmul(pb[4][:, 128:256], qfm[:], S_b[:, h, :], start=True, stop=True), inc="pe")
        pe_all = s.cnt["pe"]
        if headops <= 3:
            s.wait("vector", "pe")
            s.add("vector", lambda e: e.tensor_copy(negd[:], pb[2][:, 0:128]))
            s.add("vector", lambda e: e.tensor_copy(x1s[:], pb[3][:, 0:128]))
            s.add("vector", lambda e: e.tensor_copy(osb[:], pb[4][:, 0:128]), inc="dv")
            for b_ in (2, 3, 4):
                s.bank_release(b_, "dv")
            return

        s.wait("vector", "pe", pe_all)
        s.wait("scalar", "pe", pe_all)
        s.add("vector", lambda e, tc=tc, h=h:
              e.tensor_scalar(negd[:], pb[2][:, 0:128], cbg[:, tc, h:h + 1], None,
                              ALU.subtract, ALU.bypass))
        s.add("vector", lambda e, tc=tc, h=h:
              e.tensor_scalar(gamCc[:], pb[2][:, 256:257], cbg[:, tc, h:h + 1], None,
                              ALU.subtract, ALU.bypass), inc="dv")
        s.wait("scalar", "dv")
        s.add("scalar", lambda e, tc=tc, h=h: e.activation(gamc[:], cbg[:, tc, h:h + 1], AF.Exp))
        s.add("scalar", lambda e: e.activation(decc[:], gamCc[:], AF.Exp), inc="ac")
        s.wait("scalar", "ac")
        s.add("scalar", lambda e: e.activation(gamCc[:], pb[2][:, 256:257], AF.Exp), inc="ac")
        s.wait("vector", "ac")
        s.add("vector", lambda e: e.tensor_tensor(dmus[:], negd[:], musf[:], ALU.add))
        s.add("vector", lambda e: e.tensor_tensor(dmui[:], negd[:], muif[:], ALU.add))
        s.add("vector", lambda e: e.tensor_tensor(dml[:], mlsf[:], negd[:], ALU.subtract), inc="dv")
        s.wait("vector", "dv")
        s.wait("scalar", "dv")
        s.add("scalar", lambda e: e.activation(dmus[:], dmus[:], AF.Exp))
        s.add("scalar", lambda e: e.activation(dmui[:], dmui[:], AF.Exp))
        s.add("scalar", lambda e: e.activation(dml[:], dml[:], AF.Exp), inc="ac")
        s.wait("vector", "ac")
        s.add("vector", lambda e: e.tensor_tensor(x1s[:], pb[3][:, 0:128], dml[:], ALU.mult), inc="dv")
        s.wait("vector", "dv")
        s.add("vector", lambda e, tc=tc, h=h:
              e.tensor_scalar(bm[:], x1s[:], cbg[:, tc, 8 + h:9 + h], -1.0, ALU.mult, ALU.mult), inc="dv")
        s.wait("vector", "dv")
        s.add("vector", lambda e: e.tensor_tensor(x1s[:], pb[3][:, 0:128], dmus[:], ALU.mult), inc="dv")
        s.wait("vector", "dv")
        s.add("vector", lambda e: e.tensor_tensor(bmt[:], x1s[:], pb[2][:, 128:256], ALU.mult))
        s.add("vector", lambda e: e.tensor_tensor(mqt[:], pb[3][:, 128:256], dmui[:], ALU.mult), inc="dv")
        s.bank_release(3, "dv")
        s.bank_release(2, "dv")
        s.add("vector", lambda e: e.tensor_scalar_mul(x1s[:], pb[4][:, 0:128], gamc[:]), inc="dv")
        s.wait("vector", "dv")
        s.add("vector", lambda e, tc=tc, h=h:
              e.tensor_tensor(x1s[:], VB[:, tc, h * 128:(h + 1) * 128], x1s[:], ALU.subtract), inc="dv")
        s.wait("vector", "dv")
        s.add("vector", lambda e, tc=tc, h=h:
              e.tensor_scalar_mul(u0[:], x1s[:], cbg[:, tc, 8 + h:9 + h]))
        s.add("vector", lambda e, tc=tc, h=h:
              e.tensor_scalar_mul(kdec[:], KB[:, tc, h * 128:(h + 1) * 128], decc[:]), inc="dv")
        if headops <= 4:
            return
        if headops == 41:
            s.pe_acquire(5)
            s.wait("tensor", "dv")
            s.add("tensor", lambda e: e.matmul(pb[5][:, 256:384], bmt[:], u0[:], start=True, stop=True), inc="pe")
            s.wait("vector", "pe")
            s.add("vector", lambda e: e.tensor_tensor(u1[:], pb[5][:, 256:384], u0[:], ALU.add), inc="dv")
            s.bank_release(5, "dv")
            return
        if headops == 46:
            s.pe_acquire(5)
            s.wait("tensor", "dv")
            s.add("tensor", lambda e: e.matmul(pb[5][:, 0:128], bmt[:], bm[:], start=True, stop=True))
            s.add("tensor", lambda e: e.matmul(pb[5][:, 128:256], bm[:], bmt[:], start=True, stop=True), inc="pe")
            s.wait("scalar", "pe")
            s.add("scalar", lambda e: e.activation(pp[0][:], pb[5][:, 0:128], AF.Copy))
            s.add("scalar", lambda e: e.activation(ppt[0][:], pb[5][:, 128:256], AF.Copy), inc="ac")
            s.bank_release(5, "ac")
            s.wait("tensor", "ac")
            s.pe_acquire(5)
            s.add("tensor", lambda e: e.matmul(pb[5][:, 256:384], ppt[0][:], pp[0][:], start=True, stop=True), inc="pe")
            s.wait("vector", "pe")
            s.add("vector", lambda e: e.tensor_tensor(u1[:], pb[5][:, 256:384], u0[:], ALU.add), inc="dv")
            s.bank_release(5, "dv")
            return
        if headops == 47:
            s.pe_acquire(5)
            s.wait("tensor", "dv")
            s.add("tensor", lambda e: e.matmul(pb[5][:, 256:384], bmt[:], u0[:], start=True, stop=True), inc="pe")
            s.wait("vector", "pe")
            s.add("vector", lambda e: e.tensor_tensor(u1[:], pb[5][:, 256:384], u0[:], ALU.add), inc="dv")
            s.bank_release(5, "dv")
            s.pe_acquire(5)
            s.wait("tensor", "dv")
            s.add("tensor", lambda e: e.matmul(pb[5][:, 256:384], bmt[:], u1[:], start=True, stop=True), inc="pe")
            s.wait("vector", "pe")
            s.add("vector", lambda e: e.tensor_tensor(u0[:], pb[5][:, 256:384], u1[:], ALU.add), inc="dv")
            s.bank_release(5, "dv")
            return
        if headops == 42:
            s.pe_acquire(5)
            s.wait("tensor", "dv")
            s.add("tensor", lambda e: e.matmul(pb[5][:, 0:128], bmt[:], bm[:], start=True, stop=True))
            s.add("tensor", lambda e: e.matmul(pb[5][:, 128:256], bm[:], bmt[:], start=True, stop=True), inc="pe")
            s.wait("scalar", "pe")
            s.add("scalar", lambda e: e.activation(pp[0][:], pb[5][:, 0:128], AF.Copy))
            s.add("scalar", lambda e: e.activation(ppt[0][:], pb[5][:, 128:256], AF.Copy), inc="ac")
            s.bank_release(5, "ac")
            s.wait("tensor", "ac")
            return

        ucur, unext = u0, u1
        pcur, pcurt = bm, bmt
        nlev = 7
        s.wait("tensor", "dv")
        for lev in range(nlev):
            s.pe_acquire(3)
            s.add("tensor", lambda e, ucur=ucur:
                  e.matmul(pb[3][:, 256:384], eye_b[:], ucur[:], start=True, stop=False))
            s.add("tensor", lambda e, pcurt=pcurt, ucur=ucur:
                  e.matmul(pb[3][:, 256:384], pcurt[:], ucur[:], start=False, stop=True),
                  inc="pe" if lev == nlev - 1 else None)
            if lev < nlev - 1:
                s.pe_acquire(5)
                s.add("tensor", lambda e, pcur=pcur, pcurt=pcurt:
                      e.matmul(pb[5][:, 0:128], pcurt[:], pcur[:], start=True, stop=True))
                s.add("tensor", lambda e, pcur=pcur, pcurt=pcurt:
                      e.matmul(pb[5][:, 128:256], pcur[:], pcurt[:], start=True, stop=True), inc="pe")
            s.wait("scalar", "pe")
            s.add("scalar", lambda e, unext=unext: e.activation(unext[:], pb[3][:, 256:384], AF.Copy), inc="ac")
            s.bank_release(3, "ac")
            if lev < nlev - 1:
                pnext, pnextt = pp[lev % 2], ppt[lev % 2]
                s.add("scalar", lambda e, pnext=pnext: e.activation(pnext[:], pb[5][:, 0:128], AF.Copy))
                s.add("scalar", lambda e, pnextt=pnextt: e.activation(pnextt[:], pb[5][:, 128:256], AF.Copy), inc="ac")
                s.bank_release(5, "ac")
                pcur, pcurt = pnext, pnextt
            s.wait("tensor", "ac")
            ucur, unext = unext, ucur
        if headops <= 5:
            return

        s.pe_acquire(6)
        s.wait("tensor", "dv")
        s.add("tensor", lambda e, ucur=ucur: e.matmul(pb[6][:, 0:128], mqt[:], ucur[:], start=True, stop=True))
        s.add("tensor", lambda e, ucur=ucur: e.matmul(pb[6][:, 128:256], kdec[:], ucur[:], start=True, stop=True), inc="pe")
        s.wait("scalar", "pe")
        s.add("scalar", lambda e: e.activation(osb[:], pb[6][:, 0:128], AF.Copy), inc="ac")
        s.wait("vector", "ac")
        s.add("vector", lambda e, h=h, otm=otm:
              e.scalar_tensor_tensor(otm[:, h * 128:(h + 1) * 128], pb[4][:, 128:256], gamc[:],
                                     osb[:], ALU.mult, ALU.add), inc="dv")
        s.bank_release(4, "dv")
        s.add("vector", lambda e, h=h:
              e.scalar_tensor_tensor(S_f[:, h, :], S_f[:, h, :], gamCc[:],
                                     pb[6][:, 128:256], ALU.mult, ALU.add), inc="dv")
        s.bank_release(6, "dv")
        s.wait("scalar", "dv")
        s.add("scalar", lambda e, h=h: e.activation(S_b[:, h, :], S_f[:, h, :], AF.Copy), inc="ac")
        s.wait("tensor", "ac")

    def l2norm_tensor(tens, tc):
        s.add("scalar", lambda e, tc=tc, tens=tens:
              e.activation(l2q[:], tens[:, tc, :], AF.Square), inc="ac")
        s.wait("vector", "ac")
        for h in range(H):
            s.add("vector", lambda e, h=h:
                  e.reduce_sum(l2s[:, h:h + 1], l2q[:, h * 128:(h + 1) * 128], AX.X),
                  inc="dv" if h == H - 1 else None)
        s.wait("scalar", "dv")
        s.add("scalar", lambda e: e.activation(l2s[:], l2s[:], AF.Ln, bias=eps6c[:]), inc="ac")
        s.wait("scalar", "ac")
        s.add("scalar", lambda e: e.activation(l2s[:], l2s[:], AF.Exp, scale=-0.5), inc="ac")
        s.wait("vector", "ac")
        for h in range(H):
            s.add("vector", lambda e, h=h, tc=tc, tens=tens:
                  e.tensor_scalar_mul(tens[:, tc, h * 128:(h + 1) * 128],
                                      tens[:, tc, h * 128:(h + 1) * 128], l2s[:, h:h + 1]),
                  inc="dv" if h == H - 1 else None)
        s.wait("scalar", "dv")
        s.wait("tensor", "dv")

    def o_post(tc, otm):
        s.wait("scalar", "dv")
        s.add("scalar", lambda e, otm=otm: e.activation(l2q[:], otm[:], AF.Square), inc="ac")
        s.wait("vector", "ac")
        for h in range(H):
            s.add("vector", lambda e, h=h:
                  e.reduce_sum(l2s[:, h:h + 1], l2q[:, h * 128:(h + 1) * 128], AX.X),
                  inc="dv" if h == H - 1 else None)
        s.wait("scalar", "dv")
        s.add("scalar", lambda e: e.activation(l2s[:], l2s[:], AF.Ln, bias=eps6c[:], scale=1.0 / HD), inc="ac")
        s.wait("scalar", "ac")
        s.add("scalar", lambda e: e.activation(l2s[:], l2s[:], AF.Exp, scale=-0.5), inc="ac")
        s.wait("vector", "ac")
        for h in range(H):
            s.add("vector", lambda e, h=h, otm=otm:
                  e.tensor_scalar_mul(otm[:, h * 128:(h + 1) * 128],
                                      otm[:, h * 128:(h + 1) * 128], l2s[:, h:h + 1]), inc="dv")
            s.wait("vector", "dv")
            s.add("vector", lambda e, h=h, otm=otm:
                  e.tensor_tensor(otm[:, h * 128:(h + 1) * 128],
                                  otm[:, h * 128:(h + 1) * 128], onorm_bc[:], ALU.mult), inc="dv")
            s.wait("vector", "dv")
            s.add("vector", lambda e, h=h, tc=tc, otm=otm:
                  e.tensor_tensor(l2q[:, h * 128:(h + 1) * 128],
                                  otm[:, h * 128:(h + 1) * 128],
                                  GB[:, tc, h * 128:(h + 1) * 128], ALU.mult),
                  inc="dv" if h == H - 1 else None)
        s.wait("scalar", "dv")
        s.add("scalar", lambda e, tc=tc: e.activation(VB[:, tc, :], l2q[:], AF.Copy), inc="ac")
        s.wait("tensor", "ac")
        for dc in range(DC):
            s.pe_acquire(7)
            s.add("tensor", lambda e, tc=tc, dc=dc:
                  e.transpose(pt[:, 256:384], VB[:, tc, dc * 128:(dc + 1) * 128], eye_b[:]), inc="pe")
            s.wait("scalar", "pe")
            s.add("scalar", lambda e, tc=tc, dc=dc:
                  e.activation(OFM[:, dc, tc * 128:(tc + 1) * 128], pt[:, 256:384], AF.Copy), inc="ac")
            s.bank_release(7, "ac")

    def attn_phase():
        bav1 = s.dma("gpsimd", lambda e: e.dma_start(
            BAW[:, :, 0:8], bw.ap().rearrange("(dc p) e -> p dc e", p=128)), "dw")
        bav2 = s.dma("gpsimd", lambda e: e.dma_start(
            BAW[:, :, 8:16], aw.ap().rearrange("(dc p) e -> p dc e", p=128)), "dw")
        s.add("gpsimd", lambda e: e.memset(S_f[:], 0.0))
        s.add("gpsimd", lambda e: e.memset(S_b[:], 0.0), inc="gp")
        s.wait("tensor", "gp")

        for ti in range(p2_tiles):
            t0 = ti * TILE
            a, b = t0, t0 + TILE
            pe_sq = {}
            s.pe_acquire(2)
            s.wait("scalar", "dv")
            for dc in range(DC):
                sqb = sq0 if dc % 2 == 0 else sq1
                if dc >= 2:
                    s.wait("scalar", "pe", pe_sq[dc - 2])
                s.add("scalar", lambda e, dc=dc, a=a, b=b, sqb=sqb:
                      e.activation(sqb[:], ACC[:, dc, a:b], AF.Square), inc="ac")
                s.wait("tensor", "ac")
                pe_sq[dc] = s.add("tensor", lambda e, dc=dc, sqb=sqb:
                                  e.matmul(pb[2][0:1, 0:512], ones1[:], sqb[:],
                                           start=(dc == 0), stop=(dc == DC - 1)), inc="pe")
            s.wait("vector", "pe")
            s.add("vector", lambda e:
                  e.tensor_scalar(rowt[0:1, 0:512], pb[2][0:1, 0:512], 1.0 / D, None,
                                  ALU.mult, ALU.bypass), inc="dv")
            s.bank_release(2, "dv")
            s.wait("scalar", "dv")
            s.add("scalar", lambda e:
                  e.activation(rowt[0:1, 0:512], rowt[0:1, 0:512], AF.Ln, bias=eps6[:]))
            s.add("scalar", lambda e:
                  e.activation(rowt[0:1, 0:512], rowt[0:1, 0:512], AF.Exp, scale=-0.5), inc="ac")
            s.pe_acquire(4)
            s.wait("tensor", "ac")
            s.add("tensor", lambda e:
                  e.matmul(pb[4][:, 0:512], onesr_f[:], rowt[0:1, 0:512], start=True, stop=True), inc="pe")
            s.wait("vector", "pe")
            for dc in range(DC):
                s.add("vector", lambda e, dc=dc, a=a, b=b:
                      e.tensor_tensor(sq0[:], ACC[:, dc, a:b], pb[4][:, 0:512], ALU.mult))
                s.add("vector", lambda e, dc=dc:
                      e.tensor_scalar_mul(HB[:, dc, :], sq0[:], ln1c[:, dc:dc + 1]),
                      inc="dv" if dc == DC - 1 else None)
            s.bank_release(4, "dv")
            s.wait("tensor", "dv")

            proj_tensor(qw, QB, AF.Silu)
            proj_tensor(kw, KB, AF.Silu)
            proj_tensor(vw, VB, AF.Silu)
            proj_tensor(gw, GB, AF.Silu)
            s.wait("tensor", bav1[0], bav1[1])
            s.wait("tensor", bav2[0], bav2[1])
            for tc in range(CPT):
                s.pe_acquire(6)
                for dc in range(DC):
                    s.add("tensor", lambda e, dc=dc, tc=tc:
                          e.matmul(pb[6][:, 0:16], HB[:, dc, tc * 128:(tc + 1) * 128],
                                   BAW[:, dc, :], start=(dc == 0), stop=(dc == DC - 1)),
                          inc="pe" if dc == DC - 1 else None)
                s.wait("scalar", "pe")
                s.add("scalar", lambda e, tc=tc:
                      e.activation(cbg[:, tc, 8:16], pb[6][:, 0:8], AF.Tanh, scale=0.5), inc="ac")
                s.wait("vector", "ac")
                s.add("vector", lambda e, tc=tc:
                      e.tensor_scalar(cbg[:, tc, 8:16], cbg[:, tc, 8:16], 0.5, 0.5,
                                      ALU.mult, ALU.add))
                s.add("vector", lambda e, tc=tc:
                      e.tensor_tensor(cbg[:, tc, 16:24], pb[6][:, 8:16], dtb_bc[:], ALU.add), inc="dv")
                s.bank_release(6, "dv")

            for tc in range(p2_chunks):
                ci = ti * CPT + tc
                if g_skip:
                    continue
                s.wait("scalar", "dv")
                s.add("scalar", lambda e, tc=tc:
                      e.activation(sq0[:, 0:8], cbg[:, tc, 16:24], AF.Abs), inc="ac")
                s.wait("scalar", "ac")
                s.add("scalar", lambda e:
                      e.activation(sq0[:, 8:16], sq0[:, 0:8], AF.Exp, scale=-1.0), inc="ac")
                s.wait("scalar", "ac")
                s.add("scalar", lambda e:
                      e.activation(sq0[:, 24:32], sq0[:, 8:16], AF.Ln, bias=one1[:]), inc="ac")
                s.wait("vector", "ac")
                s.add("vector", lambda e, tc=tc:
                      e.tensor_scalar_max(sq0[:, 16:24], cbg[:, tc, 16:24], 0.0), inc="dv")
                s.wait("vector", "dv")
                s.add("vector", lambda e:
                      e.tensor_tensor(sq0[:, 32:40], sq0[:, 16:24], sq0[:, 24:32], ALU.add), inc="dv")
                s.wait("vector", "dv")
                s.add("vector", lambda e, tc=tc:
                      e.tensor_tensor(cbg[:, tc, 16:24], sq0[:, 32:40], nega_bc[:], ALU.mult), inc="dv")
                s.pe_acquire(6)
                s.wait("tensor", "dv")
                s.wait("tensor", "dx")
                first = (ci == 0)
                s.add("tensor", lambda e, tc=tc, first=first:
                      e.matmul(pb[6][:, 16:24], triu_f[:], cbg[:, tc, 16:24],
                               start=True, stop=first),
                      inc="pe" if first else None)
                if not first:
                    s.add("tensor", lambda e:
                          e.matmul(pb[6][:, 16:24], onesr_f[:],
                                   cb_rows[0:1, 127:1024:128],
                                   start=False, stop=True), inc="pe")
                s.wait("vector", "pe")
                s.add("vector", lambda e, tc=tc:
                      e.tensor_copy(cbg[:, tc, 0:8], pb[6][:, 16:24]), inc="dv")
                s.bank_release(6, "dv")
                s.pe_acquire(6)
                s.wait("tensor", "dv")
                s.add("tensor", lambda e, tc=tc:
                      e.transpose(pb[6][0:16, 128:256], cbg[:, tc, 0:16], eye_f[:]), inc="pe")
                s.wait("scalar", "pe")
                s.add("scalar", lambda e:
                      e.activation(cb_fm[:], pb[6][0:16, 128:256], AF.Copy), inc="ac")
                s.bank_release(6, "ac")
                s.wait("sync", "ac")
                h1 = s.dma("sync", lambda e: e.dma_start(
                    scr_cb.ap().rearrange("o (r c) -> (o r) c", r=16), cb_fm[:]), "dx")
                s.wait("sync", h1[0], h1[1])
                h2_ = s.dma("sync", lambda e: e.dma_start(cb_rows[:], scr_cb.ap()), "dx")
                s.wait("vector", h2_[0], h2_[1])
                s.wait("scalar", h2_[0], h2_[1])
                s.wait("tensor", h2_[0], h2_[1])
                s.add("vector", lambda e:
                      e.tensor_scalar_mul(rowt[0:1, 0:512], cb_rows[0:1, 1024:1536], -1.0))
                s.add("vector", lambda e:
                      e.tensor_scalar_mul(rows2[0:1, 0:512], cb_rows[0:1, 1536:2048], -1.0), inc="dv")
                s.wait("scalar", "dv")
                s.add("scalar", lambda e: e.activation(bneg_rb[0:1, 0:512], rowt[0:1, 0:512], AF.Copy))
                s.add("scalar", lambda e: e.activation(bneg_rb[0:1, 512:1024], rows2[0:1, 0:512], AF.Copy), inc="ac")
                s.wait("tensor", "ac")
                l2norm_tensor(QB, tc)
                l2norm_tensor(KB, tc)
                otm = OTM[:]
                for h in range(p2_heads):
                    emit_head(tc, ci, h, otm, headops)
                if p2_opost:
                    o_post(tc, otm)

            for si in range(2 if p2_ow else 0):
                j0 = si * 512
                slot = next_slot()
                dwv = wslab_dma(slot, lambda e, j0=j0, slot=slot:
                                e.dma_start(WRS[slot][:],
                                            ow.ap()[:, j0:j0 + 512].rearrange("(dc p) j -> p dc j", p=128)))
                s.wait("tensor", dwv[0], dwv[1])
                for jj in range(4):
                    jc = si * 4 + jj
                    bank = rot()
                    s.pe_acquire(bank)
                    for dc in range(DC):
                        s.add("tensor", lambda e, dc=dc, jj=jj, slot=slot, bank=bank:
                              e.matmul(pb[bank][:, 0:512],
                                       WRS[slot][:, dc, jj * 128:(jj + 1) * 128],
                                       OFM[:, dc, :], start=(dc == 0), stop=(dc == DC - 1)),
                              inc="pe" if dc == DC - 1 else None)
                    wslab_state[slot] = s.cnt["pe"]
                    s.wait("vector", "pe", s.cnt["pe"])
                    s.add("vector", lambda e, jc=jc, t0=t0, bank=bank:
                          e.scalar_tensor_tensor(ACC[:, jc, t0:t0 + TILE], pb[bank][:, 0:512], 0.5,
                                                 ACC[:, jc, t0:t0 + TILE], ALU.mult, ALU.add), inc="dv")
                    s.bank_release(bank, "dv")
            s.barrier()

    attn_phase()
    if upto == "p2":
        return finish(nc, s, y_h, ACC, dumps, locals())

    fr_state = {}
    FRS = (FR0, FR1)
    fr_slot = [0]
    hes_state = {}

    w2_state = {}

    def ffn_quarter(in_tile, w1_ap, w2_ap, act, b1_fn, use_we, out_scale, b2_on,
                    w2buf=None, w2_preloaded=None, w2_next=None, w1_next=None):
        if w2buf is None:
            w2buf = W2Q
        if w2_preloaded is None:
            s.wait("gpsimd", "pe")
            w2v = s.dma("gpsimd", lambda e, w2_ap=w2_ap, w2buf=w2buf: e.dma_start(
                w2buf[:], w2_ap.rearrange("(kc p) j -> p kc j", p=128)), "dw")
        else:
            w2v = w2_preloaded
        for si in range(4):
            j0 = si * 256
            pend1 = w2_state.pop("w1pending", None) if si == 0 else None
            if pend1 is not None:
                slot, dwv = pend1
                fr_slot[0] = slot ^ 1
            else:
                slot = fr_slot[0]
                fr_slot[0] ^= 1
                prev = fr_state.get(slot)
                if prev is not None:
                    s.wait("gpsimd", "pe", prev)
                dwv = s.dma("gpsimd", lambda e, j0=j0, slot=slot, w1_ap=w1_ap:
                            e.dma_start(FRS[slot][:],
                                        w1_ap[:, j0:j0 + 256].rearrange("(dc p) j -> p dc j", p=128)), "dw")
            s.wait("tensor", dwv[0], dwv[1])
            for jj in range(2):
                jc4 = si * 2 + jj
                for tt in range(NTILE):
                    a = tt * 512
                    bank = rot()
                    s.pe_acquire(bank)
                    for dc in range(DC):
                        s.add("tensor", lambda e, dc=dc, jj=jj, a=a, slot=slot, bank=bank, in_tile=in_tile:
                              e.matmul(pb[bank][:, 0:512],
                                       FRS[slot][:, dc, jj * 128:(jj + 1) * 128],
                                       in_tile[:, dc, a:a + 512],
                                       start=(dc == 0), stop=(dc == DC - 1)),
                              inc="pe" if dc == DC - 1 else None)
                    fr_state[slot] = s.cnt["pe"]
                    s.wait("scalar", "pe", s.cnt["pe"])
                    kw_ = {}
                    if b1_fn is not None:
                        kw_ = dict(bias=b1_fn(jc4))
                    if use_we:
                        hslot = (jc4 * NTILE + tt) % 2
                        hes = HES0 if hslot == 0 else HES1
                        hprev = hes_state.get(hslot)
                        if hprev is not None:
                            s.wait("scalar", "gp", hprev)
                        s.add("scalar", lambda e, hes=hes, bank=bank, kw_=kw_:
                              e.activation(hes[:], pb[bank][:, 0:512], act, **kw_), inc="ac")
                        s.bank_release(bank, "ac")
                        s.wait("gpsimd", "ac")
                        hes_state[hslot] = s.add(
                            "gpsimd", lambda e, jc4=jc4, a=a, hes=hes:
                            e.tensor_tensor(HFC[:, jc4, a:a + 512], hes[:],
                                            WEBC[:, a:a + 512], ALU.mult), inc="gp")
                    else:
                        s.add("scalar", lambda e, jc4=jc4, a=a, bank=bank, kw_=kw_:
                              e.activation(HFC[:, jc4, a:a + 512], pb[bank][:, 0:512], act, **kw_),
                              inc="ac")
                        s.bank_release(bank, "ac")
        if w1_next is not None:
            nslot = fr_slot[0]
            fr_slot[0] ^= 1
            nprev = fr_state.get(nslot)
            if nprev is not None:
                s.wait("gpsimd", "pe", nprev)
            w2_state["w1pending"] = (nslot, s.dma(
                "gpsimd", lambda e, w1_next=w1_next, nslot=nslot:
                e.dma_start(FRS[nslot][:],
                            w1_next[:, 0:256].rearrange("(dc p) j -> p dc j", p=128)), "dw"))
        if w2_next is not None:
            nbuf, nap = w2_next
            prevpe = w2_state.get(id(nbuf))
            if prevpe is not None:
                s.wait("gpsimd", "pe", prevpe)
            w2_state["pending"] = (nbuf, s.dma(
                "gpsimd", lambda e, nap=nap, nbuf=nbuf: e.dma_start(
                    nbuf[:], nap.rearrange("(kc p) j -> p kc j", p=128)), "dw"))
        s.wait("tensor", w2v[0], w2v[1])
        if use_we:
            s.wait("tensor", "gp")
        else:
            s.wait("tensor", "ac")
        for tt in range(NTILE):
            a = tt * 512
            for oc in range(DC):
                bank = rot()
                s.pe_acquire(bank)
                for kc in range(8):
                    last = (kc == 7 and not b2_on)
                    s.add("tensor", lambda e, kc=kc, oc=oc, a=a, bank=bank, w2buf=w2buf:
                          e.matmul(pb[bank][:, 0:512], w2buf[:, kc, oc * 128:(oc + 1) * 128],
                                   HFC[:, kc, a:a + 512], start=(kc == 0),
                                   stop=(kc == 7 and not b2_on)),
                          inc="pe" if last else None)
                if b2_on:
                    s.add("tensor", lambda e, oc=oc, a=a, bank=bank:
                          e.matmul(pb[bank][:, 0:512], b2row[0:1, oc * 128:(oc + 1) * 128],
                                   werow[0:1, a:a + 512], start=False, stop=True), inc="pe")
                s.wait("vector", "pe", s.cnt["pe"])
                if out_scale is None:
                    s.add("vector", lambda e, oc=oc, a=a, bank=bank:
                          e.tensor_tensor(ACC[:, oc, a:a + 512], pb[bank][:, 0:512],
                                          ACC[:, oc, a:a + 512], ALU.add), inc="dv")
                else:
                    s.add("vector", lambda e, oc=oc, a=a, bank=bank, osc=out_scale:
                          e.scalar_tensor_tensor(ACC[:, oc, a:a + 512], pb[bank][:, 0:512], osc,
                                                 ACC[:, oc, a:a + 512], ALU.mult, ALU.add), inc="dv")
                s.bank_release(bank, "dv")
        w2_state[id(w2buf)] = s.cnt["pe"]

    def p3_phase():
        for nt0 in range(0, T, 512):
            a, b = nt0, nt0 + 512
            pe_sq = {}
            s.pe_acquire(2)
            s.wait("scalar", "dv")
            for dc in range(DC):
                sqb = sq0 if dc % 2 == 0 else sq1
                if dc >= 2:
                    s.wait("scalar", "pe", pe_sq[dc - 2])
                s.add("scalar", lambda e, dc=dc, a=a, b=b, sqb=sqb:
                      e.activation(sqb[:], ACC[:, dc, a:b], AF.Square), inc="ac")
                s.wait("tensor", "ac")
                pe_sq[dc] = s.add("tensor", lambda e, dc=dc, sqb=sqb:
                                  e.matmul(pb[2][0:1, 0:512], ones1[:], sqb[:],
                                           start=(dc == 0), stop=(dc == DC - 1)), inc="pe")
            s.wait("vector", "pe")
            s.add("vector", lambda e:
                  e.tensor_scalar(rowt[0:1, 0:512], pb[2][0:1, 0:512], 1.0 / D, None,
                                  ALU.mult, ALU.bypass), inc="dv")
            s.bank_release(2, "dv")
            s.wait("scalar", "dv")
            s.add("scalar", lambda e:
                  e.activation(rowt[0:1, 0:512], rowt[0:1, 0:512], AF.Ln, bias=eps6[:]))
            s.add("scalar", lambda e:
                  e.activation(rowt[0:1, 0:512], rowt[0:1, 0:512], AF.Exp, scale=-0.5), inc="ac")
            s.pe_acquire(4)
            s.wait("tensor", "ac")
            s.add("tensor", lambda e:
                  e.matmul(pb[4][:, 0:512], onesr_f[:], rowt[0:1, 0:512], start=True, stop=True), inc="pe")
            s.wait("vector", "pe")
            for dc in range(DC):
                s.add("vector", lambda e, dc=dc, a=a, b=b:
                      e.tensor_tensor(sq1[:], ACC[:, dc, a:b], pb[4][:, 0:512], ALU.mult))
                s.add("vector", lambda e, dc=dc, a=a, b=b:
                      e.tensor_scalar_mul(H2[:, dc, a:b], sq1[:], ln2c[:, dc:dc + 1]),
                      inc="dv" if dc == DC - 1 else None)
            s.bank_release(4, "dv")
            s.wait("tensor", "dv")
        s.barrier()
        for q in range(4):
            w1n = fcw.ap()[:, (q + 1) * 1024:(q + 2) * 1024] if q < 3 else None
            ffn_quarter(H2, fcw.ap()[:, q * 1024:(q + 1) * 1024],
                        pw.ap()[q * 1024:(q + 1) * 1024, :], AF.Silu,
                        None, False, 0.5, False, w1_next=w1n)
            s.barrier()

    p3_phase()
    if upto == "p3":
        return finish(nc, s, y_h, ACC, dumps, locals())

    def router_phase():
        for tcg in range(p4_tcg):
            s.pe_acquire(6)
            for dc in range(DC):
                s.add("tensor", lambda e, dc=dc, tcg=tcg:
                      e.matmul(pb[6][:, 0:12], XN[:, dc, tcg * 128:(tcg + 1) * 128],
                               rlw[:, dc, :], start=(dc == 0), stop=(dc == DC - 1)),
                      inc="pe" if dc == DC - 1 else None)
            s.wait("vector", "pe")
            s.add("vector", lambda e: e.tensor_tensor(sq0[:, 0:12], pb[6][:, 0:12], rlb_bc[:], ALU.add), inc="dv")
            s.bank_release(6, "dv")
            s.wait("vector", "dv")
            s.add("vector", lambda e: e.reduce_max(l2s[:, 0:1], sq0[:, 8:12], AX.X), inc="dv")
            s.wait("vector", "dv")
            s.add("vector", lambda e:
                  e.tensor_scalar(sq0[:, 16:20], sq0[:, 8:12], l2s[:, 0:1], None,
                                  ALU.subtract, ALU.bypass), inc="dv")
            s.wait("scalar", "dv")
            s.add("scalar", lambda e: e.activation(sq0[:, 16:20], sq0[:, 16:20], AF.Exp), inc="ac")
            s.wait("vector", "ac")
            s.add("vector", lambda e: e.reduce_sum(l2s[:, 1:2], sq0[:, 16:20], AX.X), inc="dv")
            s.wait("vector", "dv")
            s.add("vector", lambda e: e.reciprocal(l2s[:, 2:3], l2s[:, 1:2]), inc="dv")
            s.wait("vector", "dv")
            s.add("vector", lambda e:
                  e.tensor_scalar_mul(sq0[:, 20:24], sq0[:, 16:20], l2s[:, 2:3]), inc="dv")
            s.wait("vector", "dv")
            s.add("vector", lambda e: e.tensor_tensor(sq0[:, 24:27], sq0[:, 20:23], sq0[:, 21:24], ALU.add))
            s.add("vector", lambda e: e.tensor_copy(sq0[:, 27:28], sq0[:, 23:24]), inc="dv")
            s.wait("vector", "dv")
            s.add("vector", lambda e:
                  e.scalar_tensor_tensor(sq0[:, 32:36], sq0[:, 24:28], 0.1, sq0[:, 0:4],
                                         ALU.mult, ALU.add))
            s.add("vector", lambda e:
                  e.scalar_tensor_tensor(sq0[:, 36:40], sq0[:, 24:28], 0.1, sq0[:, 4:8],
                                         ALU.mult, ALU.add), inc="dv")
            s.wait("vector", "dv")
            s.add("vector", lambda e: e.reduce_max(l2s[:, 3:4], sq0[:, 32:40], AX.X), inc="dv")
            s.wait("vector", "dv")
            s.add("vector", lambda e:
                  e.tensor_scalar(sq0[:, 32:40], sq0[:, 32:40], l2s[:, 3:4], None,
                                  ALU.subtract, ALU.bypass), inc="dv")
            s.wait("scalar", "dv")
            s.add("scalar", lambda e: e.activation(sq0[:, 32:40], sq0[:, 32:40], AF.Exp), inc="ac")
            s.wait("vector", "ac")
            s.add("vector", lambda e: e.reduce_sum(l2s[:, 4:5], sq0[:, 32:40], AX.X), inc="dv")
            s.wait("vector", "dv")
            s.add("vector", lambda e: e.reciprocal(l2s[:, 5:6], l2s[:, 4:5]), inc="dv")
            s.wait("vector", "dv")
            s.add("vector", lambda e:
                  e.tensor_scalar_mul(sq0[:, 40:48], sq0[:, 32:40], l2s[:, 5:6]), inc="dv")
            s.wait("vector", "dv")
            s.add("vector", lambda e: e.reduce_max(l2s[:, 6:7], sq0[:, 40:48], AX.X), inc="dv")
            s.wait("vector", "dv")
            s.add("vector", lambda e:
                  e.tensor_scalar(sq0[:, 48:56], sq0[:, 40:48], l2s[:, 6:7], None,
                                  ALU.is_ge, ALU.bypass), inc="dv")
            s.wait("vector", "dv")
            s.add("vector", lambda e:
                  e.scalar_tensor_tensor(sq0[:, 56:64], sq0[:, 48:56], NEG, sq0[:, 40:48],
                                         ALU.mult, ALU.add), inc="dv")
            s.wait("vector", "dv")
            s.add("vector", lambda e: e.reduce_max(l2s[:, 7:8], sq0[:, 56:64], AX.X), inc="dv")
            s.wait("vector", "dv")
            s.add("vector", lambda e:
                  e.tensor_scalar(sq0[:, 48:56], sq0[:, 40:48], l2s[:, 7:8], None,
                                  ALU.is_ge, ALU.bypass), inc="dv")
            s.wait("vector", "dv")
            s.add("vector", lambda e:
                  e.tensor_tensor(sq0[:, 56:64], sq0[:, 40:48], sq0[:, 48:56], ALU.mult), inc="dv")
            s.wait("vector", "dv")
            s.add("vector", lambda e: e.reduce_sum(l2s[:, 0:1], sq0[:, 56:64], AX.X), inc="dv")
            s.wait("vector", "dv")
            s.add("vector", lambda e: e.reciprocal(l2s[:, 1:2], l2s[:, 0:1]), inc="dv")
            s.wait("vector", "dv")
            s.add("vector", lambda e, tcg=tcg:
                  e.tensor_scalar_mul(wetm[:, tcg, :], sq0[:, 56:64], l2s[:, 1:2]), inc="dv")
            s.pe_acquire(6)
            s.wait("tensor", "dv")
            s.add("tensor", lambda e, tcg=tcg:
                  e.transpose(pb[6][0:8, 128:256], wetm[:, tcg, :], eye_f[:]), inc="pe")
            s.wait("scalar", "pe")
            s.add("scalar", lambda e, tcg=tcg:
                  e.activation(wefm[:, tcg * 128:(tcg + 1) * 128], pb[6][0:8, 128:256], AF.Copy), inc="ac")
            s.bank_release(6, "ac")
        s.wait("sync", "ac")
        s.dma("sync", lambda e: e.dma_start(scr_we.ap(), wefm[:]), "dx")
        s.barrier()

    def we_emit(ei):
        wv = s.dma("sync", lambda e, ei=ei: e.dma_start(werow[:], scr_we.ap()[ei:ei + 1, :]), "dx")
        for ntt in range(4):
            bank = rot()
            s.pe_acquire(bank)
            s.wait("tensor", wv[0], wv[1])
            s.add("tensor", lambda e, ntt=ntt, bank=bank:
                  e.matmul(pb[bank][:, 0:512], onesr_b[:], werow[0:1, ntt * 512:(ntt + 1) * 512],
                           start=True, stop=True), inc="pe")
            s.wait("scalar", "pe")
            s.add("scalar", lambda e, ntt=ntt, bank=bank:
                  e.activation(WEBC[:, ntt * 512:(ntt + 1) * 512], pb[bank][:, 0:512], AF.Copy), inc="ac")
            s.bank_release(bank, "ac")
        bv1 = s.dma("gpsimd", lambda e, ei=ei: e.dma_start(b1col[:], b1t.ap()[ei]), "dw")
        bv2 = s.dma("gpsimd", lambda e, ei=ei: e.dma_start(b2row[:], b2.ap()[ei:ei + 1, :]), "dw")
        s.wait("scalar", bv1[0], bv1[1])
        s.wait("tensor", bv2[0], bv2[1])
        s.wait("gpsimd", "ac")

    router_phase()
    w2bufs = (W2Q, W2Q1)
    qlist = [(ei, q) for ei in range(p4_experts) for q in range(4)]
    for qi, (ei, q) in enumerate(qlist):
        if q == 0:
            we_emit(ei)
        w2ap = w2.ap()[ei][q * 1024:(q + 1) * 1024, :]
        pend = w2_state.pop("pending", None)
        w2buf = w2bufs[qi % 2]
        pre = None
        if pend is not None and pend[0] is w2buf:
            pre = pend[1]
        nxt = None
        w1n = None
        if qi + 1 < len(qlist):
            nei, nq = qlist[qi + 1]
            nxt = (w2bufs[(qi + 1) % 2], w2.ap()[nei][nq * 1024:(nq + 1) * 1024, :])
            w1n = w1.ap()[nei][:, nq * 1024:(nq + 1) * 1024]
        ffn_quarter(XN, w1.ap()[ei][:, q * 1024:(q + 1) * 1024], w2ap, AF.Gelu,
                    (lambda jc4, q=q: b1col[:, q * 8 + jc4:q * 8 + jc4 + 1]),
                    True, None, q == 0, w2buf=w2buf, w2_preloaded=pre, w2_next=nxt, w1_next=w1n)
        if q == 3 or qi == len(qlist) - 1:
            s.barrier()
        else:
            s.barrier()

    if upto == "p4":
        return finish(nc, s, y_h, ACC, dumps, locals())

    dv_slot = {}
    for dc in range(DC):
        slot = dc % 2
        prev = dv_slot.get(slot)
        if prev is not None:
            s.wait("sync", "dv", prev)
        xv = s.dma("sync", lambda e, dc=dc, slot=slot:
                   e.dma_start(XR[:, slot, :], xT.ap()[dc * 128:(dc + 1) * 128, :]), "dx")
        s.wait("vector", xv[0], xv[1])
        dv_slot[slot] = s.add("vector", lambda e, dc=dc, slot=slot:
                              e.tensor_tensor(ACC[:, dc, :], ACC[:, dc, :], XR[:, slot, :],
                                              ALU.add), inc="dv")

    return finish(nc, s, y_h, ACC, dumps, locals())


def finish(nc, s, y_h, ACC, dumps, env):
    s.barrier()
    hs = []
    for dc in range(DC):
        hs.append(s.dma("sync", lambda e, dc=dc:
                        e.dma_start(y_h.ap()[dc * 128:(dc + 1) * 128, :], ACC[:, dc, :]), "dx"))
    for name in dumps:
        tile = env[name]
        sh = list(tile.shape)
        flat = [sh[0], int(np.prod(sh[1:]))]
        dh = nc.dram_tensor("dump_" + name, flat, F32, kind="ExternalOutput")
        ap = tile[:]
        if len(sh) == 3:
            ap = ap.rearrange("p a b -> p (a b)")
        if tile.dtype == F32:
            hs.append(s.dma("sync", lambda e, dh=dh, ap=ap: e.dma_start(dh.ap(), ap), "dx"))
        else:
            DUMPF = env["DUMPF"]
            for hh in hs:
                s.wait("scalar", hh[0], hh[1])
            s.add("scalar", lambda e, ap=ap, w=flat[1], DUMPF=DUMPF:
                  e.activation(DUMPF[:, 0:w], ap, AF.Copy), inc="ac")
            s.wait("sync", "ac")
            hs.append(s.dma("sync", lambda e, dh=dh, w=flat[1], DUMPF=DUMPF:
                            e.dma_start(dh.ap(), DUMPF[:, 0:w]), "dx"))
    for fam in list(s.dma_fams):
        s.wait_fam("sync", fam)

    with nc.Block() as block:
        @block.tensor
        def _(e):
            for f in s.q["tensor"]:
                f(e)

        @block.vector
        def _(e):
            for f in s.q["vector"]:
                f(e)

        @block.scalar
        def _(e):
            for f in s.q["scalar"]:
                f(e)

        @block.gpsimd
        def _(e):
            for f in s.q["gpsimd"]:
                f(e)

        @block.sync
        def _(e):
            for f in s.q["sync"]:
                f(e)

    return nc


def make_in_maps(inputs):
    x = np.asarray(inputs["L0_attn_x"], np.float32)
    B = x.shape[0]
    g = lambda k: np.ascontiguousarray(np.asarray(inputs[k], np.float32))
    b1 = np.ascontiguousarray(g("L0_ffn_b1").reshape(8, 32, 128).transpose(0, 2, 1))
    shared = dict(
        qw=g("L0_attn_qw"), kw=g("L0_attn_kw"), vw=g("L0_attn_vw"),
        gw=g("L0_attn_gw"), ow=g("L0_attn_ow"),
        bw=g("L0_attn_bw"), aw=g("L0_attn_aw"),
        fcw=g("L0_attn_fcw"), pw=g("L0_attn_pw"),
        w1=g("L0_ffn_w1"), w2=g("L0_ffn_w2"),
        b1t=b1, b2=g("L0_ffn_b2"),
        ln1c=np.ascontiguousarray(g("L0_attn_ln1_w").reshape(8, 128).T),
        ln2c=np.ascontiguousarray(g("L0_attn_ln2_w").reshape(8, 128).T),
        onorm_r=g("L0_attn_onorm_w").reshape(1, 128),
        alog_r=g("L0_attn_A_log").reshape(1, 8),
        dtb_r=g("L0_attn_dt_bias").reshape(1, 8),
        rwt=np.ascontiguousarray(g("L0_ffn_router_w").T),
        lwt=np.ascontiguousarray(g("L0_ffn_lang_w").T),
        rb_r=g("L0_ffn_router_b").reshape(1, 8),
        lb_r=g("L0_ffn_lang_b").reshape(1, 4),
    )
    return [dict(shared, xT=np.ascontiguousarray(x[b].T)) for b in range(B)]


def kernel(**inputs):
    nc = build("full")
    in_maps = make_in_maps(inputs)
    res = run_bass_kernel_spmd(nc, in_maps, core_ids=list(range(len(in_maps))))
    out = np.stack([np.ascontiguousarray(np.asarray(r["y"]).T) for r in res.results], axis=0)
    return out.astype(np.float32)
